# version 34
# baseline (speedup 1.0000x reference)
import numpy as np
from contextlib import ExitStack
import concourse.bass as bass
import concourse.mybir as mybir
from concourse.bass_utils import run_bass_kernel_spmd

F32 = mybir.dt.float32
BF16 = mybir.dt.bfloat16
AF = mybir.ActivationFunctionType
ALU = mybir.AluOpType

D_MODEL = 1024
D_IN = 2048
NST = 16
DT_RANK = 64
T = 4096
G_M = D_MODEL // 128
G_I = D_IN // 128
EPS = 1e-5

TC1 = 512
NCH1 = T // TC1
TC2 = 256
NCH2 = T // TC2
SEG = TC2 + 1
FMRG = NST * SEG

_EMIT = ("p1", "p2", "p3")
_DBG = False


class Ctr:
    def __init__(self, es, nc, name):
        self.h = es.enter_context(nc.semaphore(name))
        self.v = 0


class Prog:
    ENGS = ("sync", "tensor", "vector", "scalar", "gpsimd")

    def __init__(self):
        self.q = {k: [] for k in self.ENGS}

    def add(self, eng, f, inc=None, k=1):
        if inc is not None:
            inc.v += k
            sem = inc.h
            self.q[eng].append(lambda e, f=f, sem=sem, k=k: f(e).then_inc(sem, k))
            return inc.v
        self.q[eng].append(f)
        return None

    def wait(self, eng, ctr, v=None):
        vv = ctr.v if v is None else v
        if vv <= 0:
            return
        self.q[eng].append(lambda e, s=ctr.h, vv=vv: e.wait_ge(s, vv))

    def barrier(self, bar):
        for eng in self.ENGS:
            self.add(eng, lambda e: e.nop(), inc=bar, k=1)
        tgt = bar.v
        for eng in self.ENGS:
            self.wait(eng, bar, tgt)

    def finalize(self, block):
        def run(e, fs):
            for f in fs:
                f(e)
        block.sync(lambda e: run(e, self.q["sync"]))
        block.tensor(lambda e: run(e, self.q["tensor"]))
        block.vector(lambda e: run(e, self.q["vector"]))
        block.scalar(lambda e: run(e, self.q["scalar"]))
        block.gpsimd(lambda e: run(e, self.q["gpsimd"]))



def emit_ffn_pass(nc, p, sems, inp, xin, xout, tag):
    ld, pe, dv, ac, st = sems["ld"], sems["pe"], sems["dv"], sems["ac"], sems["st"]
    lg = sems["lg"]
    xin_t = xin.rearrange("(g p) t -> p g t", p=128)
    xout_t = xout.rearrange("(g p) t -> p g t", p=128)

    with ExitStack() as les:
        sb = lambda name, shape, dt=F32: les.enter_context(
            nc.sbuf_tensor(tag + name, shape, dt))
        pt = lambda name, shape: les.enter_context(
            nc.psum_tensor(tag + name, shape, F32))

        wr = sb("wr", [128, G_M * D_MODEL], BF16)
        wk = sb("wk", [128, G_M * D_MODEL], BF16)
        wv = sb("wv", [128, G_M * D_MODEL], BF16)
        mxd = sb("mxd", [128, G_M * TC1], BF16)
        muq = sb("muq", [128, 32], F32)
        ones_c = sb("onc", [128, 1], F32)
        ones_cb = sb("oncb", [128, 1], BF16)
        ones_r = sb("onr", [1, 128], F32)
        idn = sb("idn", [128, 128], BF16)
        xbuf = [sb(f"xb{i}", [128, G_M * TC1], F32) for i in range(2)]
        xnx = [sb(f"xnx{i}", [128, G_M * (TC1 + 1)], BF16) for i in range(2)]
        xsq = sb("xsq", [128, G_M * TC1], BF16)
        xrb = sb("xrb", [128, G_M * TC1], BF16)
        xkb = sb("xkb", [128, G_M * TC1], BF16)
        rb = sb("rb", [128, G_M * TC1], BF16)
        kab = sb("kab", [128, G_M * TC1], BF16)
        rowm = sb("rowm", [1, TC1], F32)
        rowv = sb("rowv", [1, TC1], F32)
        rowv2 = sb("rowv2", [1, TC1], F32)
        tmp1 = sb("tmp1", [128, TC1], F32)

        ps_s = pt("ps_s", [1, TC1])
        ps_q = pt("ps_q", [1, TC1])
        ps_bc = pt("ps_bc", [128, 2 * TC1])
        ps_pr = [pt(f"ps_pr{i}", [128, TC1]) for i in range(4)]

        xnx3 = [x.ap().rearrange("p (g t) -> p g t", g=G_M) for x in xnx]

        p.add("gpsimd", lambda g: g.dma_start(
            out=wr[:], in_=inp["wrT"].rearrange("(g p) d -> p g d", p=128)), inc=lg, k=16)
        p.add("gpsimd", lambda g: g.dma_start(
            out=wk[:], in_=inp["wkT"].rearrange("(g p) d -> p g d", p=128)), inc=lg, k=16)
        p.add("gpsimd", lambda g: g.dma_start(
            out=wv[:], in_=inp["wvT"].rearrange("(g p) d -> p g d", p=128)), inc=lg, k=16)
        p.add("gpsimd", lambda g: g.dma_start(out=idn[:], in_=inp["identity"]), inc=lg, k=16)
        p.add("gpsimd", lambda g: g.dma_start(out=ones_cb[:], in_=inp["ones_c"]), inc=lg, k=16)
        p.add("sync", lambda s: s.dma_start(out=muq[:, 0:16], in_=inp["mu"]), inc=ld, k=16)
        p.add("sync", lambda s: s.dma_start(out=ones_c[:], in_=inp["ones_c"]), inc=ld, k=16)
        p.add("sync", lambda s: s.dma_start(out=ones_r[:], in_=inp["ones_r"]), inc=ld, k=16)
        cons = sb("cons", [128, 4], F32)
        v_ld0 = ld.v
        v_lg0 = lg.v

        p.wait("vector", ld, v_ld0)
        p.wait("vector", lg, v_lg0)
        p.add("vector", lambda v: v.tensor_scalar(
            out=muq[:, 16:32], in0=muq[:, 0:16], scalar1=-1.0, scalar2=1.0,
            op0=ALU.mult, op1=ALU.add))
        p.add("vector", lambda v: v.memset(cons[:, 2:3], EPS))
        p.add("vector", lambda v: v.memset(cons[:, 3:4], -0.5))
        p.add("vector", lambda v: v.memset(xnx3[0][:, :, 0:1], 0.0))
        p.add("vector", lambda v: v.memset(xnx3[1][:, :, 0:1], 0.0), inc=dv)
        v_prep = dv.v

        F = {"ld": {}, "sumq": {}, "rowrd": {}, "bcrd": {}, "apply": {},
             "mixmm": {}, "mixr": {}, "rkmm": {}, "vmm": {}, "gates": {},
             "store": {}, "ld_h": {}, "store_h": {}, "gate_h": {},
             "mixev": {}, "rkev": {}, "sq": {}}

        def ffn_load(c):
            if c >= 2:
                p.wait("sync", st, F["store_h"][c - 2])
            p.add("sync", lambda s, c=c: s.dma_start(
                out=xbuf[c % 2][:, 0:4 * TC1],
                in_=xin_t[:, 0:4, c * TC1:(c + 1) * TC1]), inc=ld, k=16)
            F["ld_h"][c] = ld.v
            if c >= 2:
                p.wait("sync", st, F["store"][c - 2])
            p.add("sync", lambda s, c=c: s.dma_start(
                out=xbuf[c % 2][:, 4 * TC1:8 * TC1],
                in_=xin_t[:, 4:8, c * TC1:(c + 1) * TC1]), inc=ld, k=16)
            F["ld"][c] = ld.v

        def ffn_ln(c):
            xb = xbuf[c % 2]
            xo3 = xnx3[c % 2]
            p.wait("tensor", ld, F["ld_h"][c])
            if c == 0:
                p.wait("tensor", dv, v_prep)
                p.wait("tensor", lg, v_lg0)
            else:
                p.wait("tensor", dv, F["rowrd"][c - 1])
            for g in range(G_M):
                if g == 4:
                    p.wait("tensor", ld, F["ld"][c])
                p.add("tensor", lambda t, g=g, xb=xb: t.matmul(
                    out=ps_s[:], lhsT=ones_c[:], rhs=xb[:, g * TC1:(g + 1) * TC1],
                    start=(g == 0), stop=(g == G_M - 1)),
                    inc=(pe if g == G_M - 1 else None))
            p.wait("scalar", ld, F["ld"][c])
            if c >= 1:
                p.wait("scalar", pe, F["sumq"][c - 1])
            for g in range(G_M):
                p.add("scalar", lambda s, g=g, xb=xb: s.activation(
                    out=xsq[:, g * TC1:(g + 1) * TC1],
                    in_=xb[:, g * TC1:(g + 1) * TC1], func=AF.Square),
                    inc=(ac if g == G_M - 1 else None))
            v_xsq = ac.v
            p.wait("tensor", ac, v_xsq)
            for g in range(G_M):
                p.add("tensor", lambda t, g=g: t.matmul(
                    out=ps_q[:], lhsT=ones_cb[:], rhs=xsq[:, g * TC1:(g + 1) * TC1],
                    start=(g == 0), stop=(g == G_M - 1)),
                    inc=(pe if g == G_M - 1 else None))
            F["sumq"][c] = pe.v
            p.wait("vector", pe, F["sumq"][c])
            p.add("vector", lambda v: v.tensor_scalar(
                out=rowm[:], in0=ps_s[:], scalar1=1.0 / D_MODEL, scalar2=None,
                op0=ALU.mult))
            p.add("vector", lambda v: v.tensor_scalar(
                out=rowv[:], in0=ps_q[:], scalar1=1.0 / D_MODEL, scalar2=None,
                op0=ALU.mult))
            p.add("vector", lambda v: v.scalar_tensor_tensor(
                out=tmp1[0:1, :], in0=rowm[:], scalar=1.0, in1=rowm[:],
                op0=ALU.mult, op1=ALU.mult))
            p.add("vector", lambda v: v.tensor_sub(
                out=rowv[:], in0=rowv[:], in1=tmp1[0:1, :]), inc=dv)
            F["rowrd"][c] = dv.v
            p.wait("scalar", dv, dv.v)
            p.add("scalar", lambda s: s.activation(
                out=rowv2[:], in_=rowv[:], func=AF.Ln, bias=cons[0:1, 2:3]))
            p.add("scalar", lambda s: s.activation(
                out=rowv2[:], in_=rowv2[:], func=AF.Exp, scale=cons[0:1, 3:4]), inc=ac)
            v_rstd = ac.v
            p.wait("tensor", ac, v_rstd)
            if c >= 1:
                p.wait("tensor", dv, F["bcrd"][c - 1])
            p.add("tensor", lambda t: t.matmul(
                out=ps_bc[:, 0:TC1], lhsT=ones_r[:], rhs=rowm[:], start=True, stop=True))
            p.add("tensor", lambda t: t.matmul(
                out=ps_bc[:, TC1:2 * TC1], lhsT=ones_r[:], rhs=rowv2[:],
                start=True, stop=True), inc=pe)
            v_bc = pe.v
            p.wait("vector", pe, v_bc)
            if c >= 1:
                p.wait("vector", dv, F["mixmm"].get(c - 2, 0))
                p.add("vector", lambda v, c=c: v.tensor_copy(
                    out=xo3[:, :, 0:1], in_=xnx3[(c - 1) % 2][:, :, TC1:TC1 + 1]))
            for g in range(G_M):
                p.add("vector", lambda v, g=g, xb=xb: v.tensor_tensor(
                    out=tmp1[:], in0=xb[:, g * TC1:(g + 1) * TC1],
                    in1=ps_bc[:, 0:TC1], op=ALU.subtract))
                p.add("vector", lambda v, g=g, xo3=xo3: v.tensor_tensor(
                    out=xo3[:, g, 1:TC1 + 1], in0=tmp1[:],
                    in1=ps_bc[:, TC1:2 * TC1], op=ALU.mult),
                    inc=(dv if g == G_M - 1 else None))
            F["apply"][c] = dv.v
            F["bcrd"][c] = dv.v

        def ffn_mix(c):
            xo3 = xnx3[c % 2]
            p.wait("vector", dv, F["apply"][c])
            if c >= 1:
                p.wait("vector", pe, F["rkmm"][c - 1])
            for g in range(G_M):
                cur = xo3[:, g, 1:TC1 + 1]
                prv = xo3[:, g, 0:TC1]
                p.add("vector", lambda v, g=g, cur=cur, prv=prv: v.tensor_sub(
                    out=mxd[:, g * TC1:(g + 1) * TC1], in0=cur, in1=prv))
                p.add("vector", lambda v, g=g, prv=prv: v.scalar_tensor_tensor(
                    out=xrb[:, g * TC1:(g + 1) * TC1],
                    in0=mxd[:, g * TC1:(g + 1) * TC1],
                    scalar=muq[:, g:g + 1], in1=prv, op0=ALU.mult, op1=ALU.add),
                    inc=(dv if g == G_M - 1 else None))
            F["mixr"][c] = dv.v
            for g in range(G_M):
                prv = xo3[:, g, 0:TC1]
                p.add("vector", lambda v, g=g, prv=prv: v.scalar_tensor_tensor(
                    out=xkb[:, g * TC1:(g + 1) * TC1],
                    in0=mxd[:, g * TC1:(g + 1) * TC1],
                    scalar=muq[:, 8 + g:9 + g], in1=prv, op0=ALU.mult,
                    op1=ALU.add), inc=(dv if g == G_M - 1 else None))
            F["mixmm"][c] = dv.v
            F["mixev"][c] = dv.v

        def ffn_rk(c):
            pr_ev = {i: 0 for i in range(4)}
            for (wmat, src, isr) in ((wr, xrb, True), (wk, xkb, False)):
                p.wait("tensor", dv, F["mixr"][c] if isr else F["mixev"][c])
                for o in range(G_M):
                    pm = ps_pr[o % 4]
                    if pr_ev[o % 4]:
                        p.wait("tensor", ac, pr_ev[o % 4])
                    elif c >= 1:
                        p.wait("tensor", dv, F["gates"][c - 1])
                    for g in range(G_M):
                        p.add("tensor", lambda t, wmat=wmat, src=src, g=g, o=o, pm=pm: t.matmul(
                            out=pm[:],
                            lhsT=wmat[:, g * D_MODEL + o * 128:g * D_MODEL + (o + 1) * 128],
                            rhs=src[:, g * TC1:(g + 1) * TC1],
                            start=(g == 0), stop=(g == G_M - 1)),
                            inc=(pe if g == G_M - 1 else None))
                    vv = pe.v
                    p.wait("scalar", pe, vv)
                    if isr and o == 0 and c >= 1:
                        p.wait("scalar", dv, F["gates"][c - 1])
                    if (not isr) and o == 0 and c >= 1:
                        p.wait("scalar", pe, F["vmm"][c - 1])
                    fn = AF.Sigmoid if isr else AF.Relu
                    dstb = rb if isr else kab
                    p.add("scalar", lambda s, o=o, pm=pm, fn=fn, dstb=dstb: s.activation(
                        out=dstb[:, o * TC1:(o + 1) * TC1], in_=pm[:], func=fn), inc=ac)
                    pr_ev[o % 4] = ac.v
                    if not isr:
                        p.wait("vector", ac, ac.v)
                        p.add("vector", lambda v, o=o: v.tensor_tensor(
                            out=kab[:, o * TC1:(o + 1) * TC1],
                            in0=kab[:, o * TC1:(o + 1) * TC1],
                            in1=kab[:, o * TC1:(o + 1) * TC1], op=ALU.mult),
                            inc=(dv if o == G_M - 1 else None))
            F["rkmm"][c] = pe.v
            F["rkev"] = pr_ev
            F["sq"][c] = dv.v

        def ffn_v(c):
            v_ka = F["sq"][c]
            p.wait("tensor", dv, v_ka)
            pr_ev = dict(F["rkev"])
            pr_dv = {i: 0 for i in range(4)}
            xb = xbuf[c % 2]
            for o in range(G_M):
                pm = ps_pr[o % 4]
                if pr_ev[o % 4]:
                    p.wait("tensor", ac, pr_ev[o % 4])
                    pr_ev[o % 4] = 0
                if pr_dv[o % 4]:
                    p.wait("tensor", dv, pr_dv[o % 4])
                for g in range(G_M):
                    p.add("tensor", lambda t, g=g, o=o, pm=pm: t.matmul(
                        out=pm[:],
                        lhsT=wv[:, g * D_MODEL + o * 128:g * D_MODEL + (o + 1) * 128],
                        rhs=kab[:, g * TC1:(g + 1) * TC1],
                        start=(g == 0), stop=(g == G_M - 1)),
                        inc=(pe if g == G_M - 1 else None))
                vv = pe.v
                p.wait("vector", pe, vv)
                p.add("vector", lambda v, o=o, pm=pm: v.scalar_tensor_tensor(
                    out=tmp1[:], in0=rb[:, o * TC1:(o + 1) * TC1], scalar=0.5,
                    in1=pm[:], op0=ALU.mult, op1=ALU.mult))
                p.add("vector", lambda v, o=o, xb=xb: v.tensor_tensor(
                    out=xb[:, o * TC1:(o + 1) * TC1], in0=tmp1[:],
                    in1=xb[:, o * TC1:(o + 1) * TC1], op=ALU.add), inc=dv)
                pr_dv[o % 4] = dv.v
                if o == 3:
                    F["gate_h"][c] = dv.v
            F["vmm"][c] = pe.v
            F["gates"][c] = dv.v
            p.wait("sync", dv, F["gate_h"][c])
            p.add("sync", lambda s, c=c, xb=xb: s.dma_start(
                out=xout_t[:, 0:4, c * TC1:(c + 1) * TC1],
                in_=xb[:, 0:4 * TC1]), inc=st, k=16)
            F["store_h"][c] = st.v
            p.wait("sync", dv, F["gates"][c])
            p.add("sync", lambda s, c=c, xb=xb: s.dma_start(
                out=xout_t[:, 4:8, c * TC1:(c + 1) * TC1],
                in_=xb[:, 4 * TC1:8 * TC1]), inc=st, k=16)
            F["store"][c] = st.v

        ffn_load(0)
        ffn_ln(0)
        for c in range(NCH1):
            if c + 1 < NCH1:
                ffn_load(c + 1)
            ffn_mix(c)
            ffn_rk(c)
            if c + 1 < NCH1:
                ffn_ln(c + 1)
            ffn_v(c)

        p.wait("sync", st, st.v)


def emit_wcast(nc, p, sems, inp, scr, les):
    gp = sems["gp"]
    win_bf, wout_bf = scr["win_bf"], scr["wout_bf"]
    wint = inp["win_t"]
    woutt = inp["wout_t"]
    for grp in range(8):
        p.add("gpsimd", lambda g, grp=grp: g.dma_start(
            out=win_bf[grp], in_=wint[grp]), inc=gp, k=16)
    for grp in range(2):
        p.add("gpsimd", lambda g, grp=grp: g.dma_start(
            out=wout_bf[grp], in_=woutt[grp]), inc=gp, k=16)
    scr["v_wcast"] = ("gp", sems["gp"], sems["gp"].v)


def emit_mamba_pass(nc, p, sems, inp, xin, xout, scr):
    ld, pe, dv, ac, st, gp = (sems["ld"], sems["pe"], sems["dv"], sems["ac"],
                              sems["st"], sems["gp"])
    lg = sems["lg"]
    xin_t = xin.rearrange("(g p) t -> p g t", p=128)
    xout_t = xout.rearrange("(g p) t -> p g t", p=128)
    win_bf, wout_bf = scr["win_bf"], scr["wout_bf"]

    with ExitStack() as les:
        sb = lambda name, shape, dt=F32: les.enter_context(nc.sbuf_tensor("m" + name, shape, dt))
        pt = lambda name, shape: les.enter_context(nc.psum_tensor("m" + name, shape, F32))

        wx = sb("wx", [128, G_I * 96], BF16)
        wdt = sb("wdt", [64, D_IN], BF16)
        a_sb = sb("a", [128, G_I * NST], F32)
        convd = sb("convd", [128, 4 * G_I * 128], BF16)
        dd = sb("dd", [128, G_I * 128], BF16)
        convb = sb("convb", [128, G_I], F32)
        selbig = sb("selb", [33, 16 * 128 + TC2], BF16)
        selb = selbig.ap()[0:16, 0:16 * 128]
        idn = sb("idn", [128, 128], BF16)
        ones_c = sb("onc", [128, 1], F32)
        ones_cb = sb("oncb", [128, 1], BF16)
        ones_r = sb("onr", [1, 128], F32)
        cons = sb("cons2", [128, 4], F32)
        bdt_r = selbig.ap()[32:33, 0:D_IN]
        ones_rb = selbig.ap()[32:33, D_IN:D_IN + TC2]

        winb = sb("win", [128, 8 * 512], BF16)
        woutb1 = sb("wo1", [128, 4 * 512], BF16)

        xbuf = [sb(f"xb{i}", [128, G_M * TC2], F32) for i in range(2)]
        xnb = sb("xnb", [128, G_M * TC2], BF16)
        woutb = [xnb, woutb1]
        xsq = xnb
        zscr = sb("zscr", [128, 1536], F32)
        zscr2 = sb("zscr2", [128, 512], BF16)
        cvec = zscr.ap()[:, 256:320]
        dvec = zscr.ap()[:, 320:336]
        xce = sb("xce", [128, G_I * (TC2 + 3)], BF16)
        sres = [sb(f"sres{i}", [128, G_I * TC2], BF16) for i in range(2)]
        ub = [sb(f"ub{i}", [128, G_I * TC2], BF16) for i in range(2)]
        deltab = [sb(f"delta{i}", [128, G_I * TC2], BF16) for i in range(2)]
        dub = sb("dub", [128, 4 * TC2], BF16)
        dltb = sb("dltb", [64, TC2], BF16)
        bcsrc = sb("bcsrc", [16, 2 * TC2], BF16)
        bm = sb("bm", [128, FMRG], BF16)
        cm = sb("cm", [128, FMRG], BF16)
        dam2 = sb("dam2", [128, 3 * FMRG], BF16)
        dam = [dam2.ap()[:, i * FMRG:(i + 1) * FMRG] for i in range(3)]
        xpm = sb("xpm", [128, FMRG], BF16)
        hm = sb("hm", [128, FMRG], BF16)
        hcm = hm
        carry = sb("carry", [128, G_I * NST], BF16)
        tmp1 = zscr.ap()[:, 1024:1024 + TC2]

        rowm = zscr.ap()[0:1, 0:TC2]
        rowv = zscr.ap()[0:1, TC2:2 * TC2]
        rowv2 = zscr.ap()[0:1, 2 * TC2:3 * TC2]

        ps_sq = pt("ps_sq", [128, 512])
        ps_bc = pt("ps_bc", [128, 512])
        ps_a = [pt(f"ps_a{i}", [128, 512]) for i in range(2)]
        ps_b = [pt(f"ps_b{i}", [128, 512]) for i in range(2)]
        ps_y = [pt(f"ps_y{i}", [128, 512]) for i in range(2)]

        xce3 = xce.ap().rearrange("p (g t) -> p g t", g=G_I)
        bm3 = bm.ap().rearrange("p (n t) -> p n t", n=NST)
        cm3 = cm.ap().rearrange("p (n t) -> p n t", n=NST)
        dam3 = [x.rearrange("p (n t) -> p n t", n=NST) for x in dam]
        xpm3 = xpm.ap().rearrange("p (n t) -> p n t", n=NST)
        hm3 = hm.ap().rearrange("p (n t) -> p n t", n=NST)
        hcm3 = hm3

        p.add("gpsimd", lambda g: g.dma_start(
            out=wx[:], in_=inp["wx"].rearrange("(h p) c -> p h c", p=128)), inc=lg, k=16)
        p.add("gpsimd", lambda g: g.dma_start(out=wdt[:], in_=inp["wdt"]), inc=lg, k=16)
        p.add("gpsimd", lambda g: g.dma_start(out=selb, in_=inp["selb"]), inc=lg, k=16)
        p.add("gpsimd", lambda g: g.dma_start(out=idn[:], in_=inp["identity"]), inc=lg, k=16)
        p.add("gpsimd", lambda g: g.dma_start(out=ones_cb[:], in_=inp["ones_c"]), inc=lg, k=16)
        p.add("gpsimd", lambda g: g.dma_start(out=bdt_r, in_=inp["bdt_r"]), inc=lg, k=16)
        p.add("sync", lambda s: s.dma_start(out=zscr[:, 0:G_I * NST], in_=inp["a_log"]), inc=ld, k=16)
        p.add("sync", lambda s: s.dma_start(out=cvec, in_=inp["convw"]), inc=ld, k=16)
        p.add("sync", lambda s: s.dma_start(out=convb[:], in_=inp["convb"]), inc=ld, k=16)
        p.add("sync", lambda s: s.dma_start(out=dvec, in_=inp["dvec"]), inc=ld, k=16)
        p.add("sync", lambda s: s.dma_start(out=ones_c[:], in_=inp["ones_c"]), inc=ld, k=16)
        p.add("sync", lambda s: s.dma_start(out=ones_r[:], in_=inp["ones_r"]), inc=ld, k=16)
        v_ldp = ld.v
        v_lgp = lg.v

        p.wait("scalar", ld, v_ldp)
        p.add("scalar", lambda s: s.activation(
            out=a_sb[:], in_=zscr[:, 0:G_I * NST], func=AF.Exp), inc=ac)
        p.wait("vector", ac, ac.v)
        p.add("vector", lambda v: v.tensor_scalar(
            out=a_sb[:], in0=a_sb[:], scalar1=-1.0, scalar2=None, op0=ALU.mult))
        p.wait("vector", ld, v_ldp)
        p.wait("vector", lg, v_lgp)
        for tap in range(4):
            for h in range(G_I):
                dst = convd[:, (tap * G_I + h) * 128:(tap * G_I + h + 1) * 128]
                p.add("vector", lambda v, dst=dst, tap=tap, h=h: v.tensor_scalar(
                    out=dst, in0=idn[:], scalar1=cvec[:, h * 4 + tap:h * 4 + tap + 1],
                    scalar2=None, op0=ALU.mult))
        for h in range(G_I):
            dst = dd[:, h * 128:(h + 1) * 128]
            p.add("vector", lambda v, dst=dst, h=h: v.tensor_scalar(
                out=dst, in0=idn[:], scalar1=dvec[:, h:h + 1], scalar2=None,
                op0=ALU.mult))
        p.add("vector", lambda v: v.memset(ones_rb, 1.0))
        p.add("vector", lambda v: v.memset(cons[:, 0:1], -1.0))
        p.add("vector", lambda v: v.memset(cons[:, 1:2], 1.0))
        p.add("vector", lambda v: v.memset(cons[:, 2:3], EPS))
        p.add("vector", lambda v: v.memset(cons[:, 3:4], -0.5))
        p.add("vector", lambda v: v.memset(carry[:], 0.0))
        p.add("vector", lambda v: v.memset(xce3[:, :, 0:3], 0.0))
        p.add("vector", lambda v: v.memset(dam3[0][:, :, 0:1], 0.0))
        p.add("vector", lambda v: v.memset(dam3[1][:, :, 0:1], 0.0))
        p.add("vector", lambda v: v.memset(dam3[2][:, :, 0:1], 0.0), inc=dv)
        v_prep = dv.v

        v_wcast = scr["v_wcast"]

        M = {
            "store": 0,
            "xload": {},
            "apply": {},
            "win_mm": {},
            "win_half": {},
            "win_ev": {},
            "conv_ev": {},
            "wx_ev": 0,
            "bc_ev": {},
            "sp_add": {},
            "wdt_pm": {},
            "du": {},
            "ts": {},
            "exp": {},
            "xp": {},
            "scan": {},
            "hc": {},
            "y": {},
            "gate": {},
            "wout_mm": {},
            "outadd": {},
            "lnbc": {},
            "sum_rd": {},
        }

        def pre_load(c):
            if c >= 2:
                p.wait("sync", st, M["store"])
            p.add("sync", lambda s, c=c: s.dma_start(
                out=xbuf[c % 2][:], in_=xin_t[:, :, c * TC2:(c + 1) * TC2]), inc=ld, k=16)
            M["xload"][c] = ld.v

        def pre_ln_a(c):
            xb = xbuf[c % 2]
            p.wait("tensor", ld, M["xload"][c])
            if c == 0:
                p.wait("tensor", dv, v_prep)
                p.wait("tensor", lg, v_lgp)
            if c - 1 in M["sum_rd"]:
                p.wait("tensor", dv, M["sum_rd"][c - 1])
            p.wait("tensor", dv, M["outadd"].get((c - 2, "sq"), 0))
            for g in range(G_M):
                p.add("tensor", lambda t, g=g, xb=xb: t.matmul(
                    out=ps_sq[0:1, 0:TC2], lhsT=ones_c[:],
                    rhs=xb[:, g * TC2:(g + 1) * TC2],
                    start=(g == 0), stop=(g == G_M - 1)),
                    inc=(pe if g == G_M - 1 else None))
            p.wait("vector", ld, M["xload"][c])
            if c >= 1:
                p.wait("vector", pe, M["win_mm"][(c - 1, 7)])
            p.add("vector", lambda v, xb=xb: v.tensor_tensor(
                out=xsq[:], in0=xb[:], in1=xb[:], op=ALU.mult), inc=dv)
            v_sq = dv.v
            p.wait("tensor", dv, v_sq)
            for g in range(G_M):
                p.add("tensor", lambda t, g=g: t.matmul(
                    out=ps_sq[0:1, TC2:2 * TC2], lhsT=ones_cb[:],
                    rhs=xsq[:, g * TC2:(g + 1) * TC2],
                    start=(g == 0), stop=(g == G_M - 1)),
                    inc=(pe if g == G_M - 1 else None))
            M["lnsum"] = pe.v
            pre_ln_stats(c)

        def pre_ln_stats(c):
            p.wait("vector", pe, M["lnsum"])
            p.add("vector", lambda v: v.tensor_scalar(
                out=rowm, in0=ps_sq[0:1, 0:TC2], scalar1=1.0 / D_MODEL,
                scalar2=None, op0=ALU.mult))
            p.add("vector", lambda v: v.tensor_scalar(
                out=rowv, in0=ps_sq[0:1, TC2:2 * TC2], scalar1=1.0 / D_MODEL,
                scalar2=None, op0=ALU.mult))
            p.add("vector", lambda v: v.scalar_tensor_tensor(
                out=tmp1[0:1, :], in0=rowm, scalar=1.0, in1=rowm,
                op0=ALU.mult, op1=ALU.mult))
            p.add("vector", lambda v: v.tensor_sub(
                out=rowv, in0=rowv, in1=tmp1[0:1, :]), inc=dv)
            M["sum_rd"][c] = dv.v
            p.wait("scalar", dv, dv.v)
            p.add("scalar", lambda s: s.activation(
                out=rowv2, in_=rowv, func=AF.Ln, bias=cons[0:1, 2:3]))
            p.add("scalar", lambda s: s.activation(
                out=rowv2, in_=rowv2, func=AF.Exp, scale=cons[0:1, 3:4]), inc=ac)
            v_rstd = ac.v
            p.wait("tensor", ac, v_rstd)
            if c >= 2:
                p.wait("tensor", dv, M["outadd"][(c - 2, "bc")])
            p.add("tensor", lambda t: t.matmul(
                out=ps_bc[:, 0:TC2], lhsT=ones_r[:], rhs=rowm, start=True, stop=True))
            p.add("tensor", lambda t: t.matmul(
                out=ps_bc[:, TC2:2 * TC2], lhsT=ones_r[:], rhs=rowv2,
                start=True, stop=True), inc=pe)
            M["lnv_bcm"] = pe.v

        def pre_ln_b(c):
            xb = xbuf[c % 2]
            p.wait("vector", pe, M["lnv_bcm"])
            p.wait("vector", pe, M["wout_mm"].get(c - 2, 0))
            for g in range(G_M):
                p.add("vector", lambda v, g=g, xb=xb: v.tensor_tensor(
                    out=tmp1[:], in0=xb[:, g * TC2:(g + 1) * TC2],
                    in1=ps_bc[:, 0:TC2], op=ALU.subtract))
                p.add("vector", lambda v, g=g: v.tensor_tensor(
                    out=xnb[:, g * TC2:(g + 1) * TC2], in0=tmp1,
                    in1=ps_bc[:, TC2:2 * TC2], op=ALU.mult),
                    inc=(dv if g == G_M - 1 else None))
            M["apply"][c] = dv.v
            M["lnbc"][c] = dv.v

        def pre_win(c, grp):
            if c == 0 and grp == 0:
                p.wait("sync", v_wcast[1], v_wcast[2])
            if (c, grp - 1, 0) in M["win_half"]:
                p.wait("sync", pe, M["win_half"][(c, grp - 1, 0)])
            elif c >= 1:
                p.wait("sync", pe, M["win_mm"][(c - 1, 7)])
            p.add("sync", lambda s, grp=grp: s.dma_start(
                out=winb[:, 0:2048],
                in_=win_bf[grp, 0:4].rearrange("k p c -> p k c")), inc=ld, k=16)
            v_wa = ld.v
            if (c, grp - 1) in M["win_mm"]:
                p.wait("sync", pe, M["win_mm"][(c, grp - 1)])
            elif c >= 1:
                p.wait("sync", pe, M["win_mm"][(c - 1, 7)])
            p.add("sync", lambda s, grp=grp: s.dma_start(
                out=winb[:, 2048:4096],
                in_=win_bf[grp, 4:8].rearrange("k p c -> p k c")), inc=ld, k=16)
            v_wb = ld.v
            p.wait("tensor", ld, v_wa)
            if grp == 0:
                p.wait("tensor", dv, M["apply"][c])
                p.wait("scalar", dv, M["halo"])
            for o in range(4):
                oo = grp * 4 + o
                pm = [ps_a[0], ps_a[1], ps_b[0], ps_b[1]][oo % 4]
                if (c, oo - 4) in M["win_ev"]:
                    p.wait("tensor", ac, M["win_ev"][(c, oo - 4)])
                elif c >= 1 and oo < 2:
                    p.wait("tensor", ac, M["bc_last"])
                elif c >= 2 and oo in (2, 3):
                    p.wait("tensor", dv, M["outadd"][(c - 2, "b%d" % (oo - 2))])
                for k in range(4):
                    p.add("tensor", lambda t, k=k, o=o, pm=pm: t.matmul(
                        out=pm[:, 0:TC2],
                        lhsT=winb[:, k * 512 + o * 128:k * 512 + (o + 1) * 128],
                        rhs=xnb[:, k * TC2:(k + 1) * TC2],
                        start=(k == 0), stop=False),
                        inc=(pe if o == 3 and k == 3 else None))
            M["win_half"][(c, grp, 0)] = pe.v
            p.wait("tensor", ld, v_wb)
            for o in range(4):
                oo = grp * 4 + o
                pm = [ps_a[0], ps_a[1], ps_b[0], ps_b[1]][oo % 4]
                for k in range(4, 8):
                    p.add("tensor", lambda t, k=k, o=o, pm=pm: t.matmul(
                        out=pm[:, 0:TC2],
                        lhsT=winb[:, k * 512 + o * 128:k * 512 + (o + 1) * 128],
                        rhs=xnb[:, k * TC2:(k + 1) * TC2],
                        start=False, stop=(k == 7)),
                        inc=(pe if k == 7 else None))
                vv = pe.v
                p.wait("scalar", pe, vv)
                if oo < 16:
                    p.add("scalar", lambda s, oo=oo, pm=pm: s.copy(
                        out=xce3[:, oo, 3:TC2 + 3], in_=pm[:, 0:TC2]), inc=ac)
                else:
                    p.add("scalar", lambda s, oo=oo, pm=pm, c=c: s.activation(
                        out=sres[c % 2][:, (oo - 16) * TC2:(oo - 15) * TC2],
                        in_=pm[:, 0:TC2], func=AF.Silu), inc=ac)
                M["win_ev"][(c, oo)] = ac.v
            M["win_mm"][(c, grp)] = pe.v

        def pre_conv(c, hs):
            for h in hs:
                pm = ps_b[h % 2]
                if (c, h - 2) in M["conv_ev"]:
                    p.wait("tensor", ac, M["conv_ev"][(c, h - 2)])
                elif h < 2:
                    p.wait("tensor", ac, M["win_ev"][(c, 30 + h)])
                if h == 0:
                    p.wait("tensor", ac, M["win_ev"][(c, 15)])
                for tap in range(4):
                    p.add("tensor", lambda t, h=h, tap=tap, pm=pm: t.matmul(
                        out=pm[:, 0:TC2],
                        lhsT=convd[:, (tap * G_I + h) * 128:(tap * G_I + h + 1) * 128],
                        rhs=xce3[:, h, tap:tap + TC2],
                        start=(tap == 0), stop=(tap == 3)),
                        inc=(pe if tap == 3 else None))
                vv = pe.v
                p.wait("scalar", pe, vv)
                if c >= 2:
                    p.wait("scalar", pe, M["y"][(c - 2, h)])
                p.add("scalar", lambda s, h=h, pm=pm, c=c: s.activation(
                    out=ub[c % 2][:, h * TC2:(h + 1) * TC2], in_=pm[:, 0:TC2],
                    func=AF.Silu, bias=convb[:, h:h + 1]), inc=ac)
                M["conv_ev"][(c, h)] = ac.v

        def pre_wx(c):
            p.wait("tensor", ac, M["conv_ev"][(c, 15)])
            p.wait("tensor", ac, M["win_ev"][(c, 28)])
            p.wait("tensor", ac, M["win_ev"][(c, 29)])
            if c >= 1:
                p.wait("scalar", pe, M["wdt_mm_last"])
                p.wait("scalar", pe, M["bc_mm_last"])
            u = ub[c % 2]
            for k in range(G_I):
                p.add("tensor", lambda t, k=k, u=u: t.matmul(
                    out=ps_a[0][0:80, 0:TC2], lhsT=wx[:, k * 96:k * 96 + 80],
                    rhs=u[:, k * TC2:(k + 1) * TC2],
                    start=(k == 0), stop=(k == G_I - 1)),
                    inc=(pe if k == G_I - 1 else None))
            for k in range(G_I):
                p.add("tensor", lambda t, k=k, u=u: t.matmul(
                    out=ps_a[1][0:16, 0:TC2], lhsT=wx[:, k * 96 + 80:(k + 1) * 96],
                    rhs=u[:, k * TC2:(k + 1) * TC2],
                    start=(k == 0), stop=(k == G_I - 1)),
                    inc=(pe if k == G_I - 1 else None))
            v_xdbl = pe.v
            p.wait("scalar", pe, v_xdbl)
            p.add("scalar", lambda s: s.copy(out=dltb[:], in_=ps_a[0][0:64, 0:TC2]))
            p.add("scalar", lambda s: s.copy(
                out=bcsrc[:, 0:TC2], in_=ps_a[0][64:80, 0:TC2]))
            p.add("scalar", lambda s: s.copy(
                out=bcsrc[:, TC2:2 * TC2], in_=ps_a[1][0:16, 0:TC2]), inc=ac)
            M["wx_ev"] = ac.v

        def pre_bcast(c, which):
            wi, dst3 = (0, bm3) if which == "b" else (1, cm3)
            p.wait("tensor", ac, M["wx_ev"])
            if c >= 1:
                if which == "b":
                    p.wait("scalar", dv, M["xp"][(c - 1, 15)])
                else:
                    p.wait("scalar", dv, M["hc"][(c - 1, 15)])
            for n in range(NST):
                pm = [ps_a[0], ps_a[1], ps_y[0], ps_y[1]][n % 4]
                if (n % 4) in M["bc_ev"]:
                    p.wait("tensor", ac, M["bc_ev"][n % 4])
                elif n % 4 >= 2 and c >= 1:
                    p.wait("tensor", dv, M["gate"][(c - 1, G_I - 4 + n)])
                p.add("tensor", lambda t, n=n, wi=wi, pm=pm: t.matmul(
                    out=pm[:, 0:TC2], lhsT=selb[:, n * 128:(n + 1) * 128],
                    rhs=bcsrc[:, wi * TC2:(wi + 1) * TC2], start=True, stop=True), inc=pe)
                vv = pe.v
                p.wait("scalar", pe, vv)
                p.add("scalar", lambda s, n=n, dst3=dst3, pm=pm: s.copy(
                    out=dst3[:, n, 1:TC2 + 1], in_=pm[:, 0:TC2]), inc=ac)
                M["bc_ev"][n % 4] = ac.v
            M["bc_mm_last"] = pe.v
            M["bc_ev_all"][which] = ac.v
            if which == "c":
                M["bc_last"] = ac.v
                M["bc_ev"] = {}

        def _sp_regions(pi):
            if pi % 2 == 0:
                return zscr.ap()[:, 0:512], zscr.ap()[:, 512:1024]
            return zscr.ap()[:, 1024:1536], zscr2.ap()[:, 0:512]

        def pre_sp_act(c, pairs):
            for pi in pairs:
                h0 = 2 * pi
                pm = ps_b[pi % 2]
                rr, aa = _sp_regions(pi)
                p.wait("tensor", ac, M["wx_ev"])
                if (c, pi - 2) in M["wdt_pm"]:
                    p.wait("tensor", ac, M["wdt_pm"][(c, pi - 2)])
                else:
                    p.wait("tensor", ac, M["conv_ev"][(c, 14 + (pi % 2))])
                for j in range(2):
                    h = h0 + j
                    p.add("tensor", lambda t, h=h, j=j, pm=pm: t.matmul(
                        out=pm[:, j * TC2:(j + 1) * TC2],
                        lhsT=wdt[:, h * 128:(h + 1) * 128], rhs=dltb[:],
                        start=(j == 0), stop=False))
                for j in range(2):
                    h = h0 + j
                    p.add("tensor", lambda t, h=h, j=j, pm=pm: t.matmul(
                        out=pm[:, j * TC2:(j + 1) * TC2],
                        lhsT=bdt_r[0:1, h * 128:(h + 1) * 128], rhs=ones_rb,
                        start=False, stop=(j == 1)),
                        inc=(pe if j == 1 else None))
                vv = pe.v
                M["wdt_mm_last"] = pe.v
                p.wait("scalar", pe, vv)
                if (c, pi - 2) in M["sp_add"]:
                    p.wait("scalar", dv, M["sp_add"][(c, pi - 2)])
                p.add("scalar", lambda s, pm=pm, rr=rr: s.activation(
                    out=rr, in_=pm[:, 0:512], func=AF.Relu))
                p.add("scalar", lambda s, pm=pm, aa=aa: s.activation(
                    out=aa, in_=pm[:, 0:512], func=AF.Abs), inc=ac)
                M["wdt_pm"][(c, pi)] = ac.v
                p.add("scalar", lambda s, aa=aa: s.activation(
                    out=aa, in_=aa, func=AF.Exp, scale=cons[:, 0:1]))
                p.add("scalar", lambda s, aa=aa: s.activation(
                    out=aa, in_=aa, func=AF.Ln, bias=cons[:, 1:2]), inc=ac)
                M["sp_ln"][(c, pi)] = ac.v
            M["wdt_last"] = ac.v

        def pre_sp_add(c, pairs):
            for pi in pairs:
                h0 = 2 * pi
                rr, aa = _sp_regions(pi)
                p.wait("vector", ac, M["sp_ln"][(c, pi)])
                p.add("vector", lambda v, h0=h0, rr=rr, aa=aa: v.tensor_tensor(
                    out=deltab[c % 2][:, h0 * TC2:(h0 + 2) * TC2],
                    in0=aa, in1=rr, op=ALU.add), inc=dv)
                M["sp_add"][(c, pi)] = dv.v
                M["sp_add_h"][(c, h0)] = dv.v
                M["sp_add_h"][(c, h0 + 1)] = dv.v

        def pre_halo(c):
            if c > 0:
                p.add("vector", lambda v: v.tensor_copy(
                    out=xce3[:, :, 0:3], in_=xce3[:, :, TC2:TC2 + 3]), inc=dv)
            M["halo"] = dv.v

        def emit_du(c, h):
            p.wait("vector", dv, M["sp_add_h"][(c, h)])
            p.add("vector", lambda v, h=h, c=c: v.tensor_tensor(
                out=dub.ap()[:, (h % 4) * TC2:(h % 4 + 1) * TC2],
                in0=deltab[c % 2][:, h * TC2:(h + 1) * TC2],
                in1=ub[c % 2][:, h * TC2:(h + 1) * TC2], op=ALU.mult), inc=dv)
            M["du"][(c, h)] = dv.v

        def emit_dexp(c, h):
            if h >= 3:
                p.wait("scalar", dv, M["scan"][(c, h - 3)])
            elif c >= 1:
                p.wait("scalar", dv, M["scan"][(c - 1, G_I - 3 + h)])
            p.wait("scalar", dv, M["sp_add_h"][(c, h)])
            da3 = dam3[(c * G_I + h) % 3]
            for n in range(NST):
                p.add("scalar", lambda s, h=h, n=n, c=c, da3=da3: s.activation(
                    out=da3[:, n, 1:TC2 + 1],
                    in_=deltab[c % 2][:, h * TC2:(h + 1) * TC2],
                    func=AF.Exp, scale=a_sb[:, h * NST + n:h * NST + n + 1]),
                    inc=(ac if n == NST - 1 else None))
            M["exp"][(c, h)] = ac.v

        def emit_scan_iter(c, h):
            if h == 0:
                p.wait("vector", ac, M["bc_ev_all"]["b"])
            p.add("vector", lambda v, h=h: v.tensor_tensor(
                out=xpm3[:, :, 1:TC2 + 1],
                in0=dub.ap()[:, (h % 4) * TC2:(h % 4 + 1) * TC2].unsqueeze(
                    1).broadcast_to([128, NST, TC2]),
                in1=bm3[:, :, 1:TC2 + 1], op=ALU.mult))
            p.add("vector", lambda v, h=h: v.tensor_copy(
                out=xpm3[:, :, 0:1],
                in_=carry.ap()[:, h * NST:(h + 1) * NST].unsqueeze(2)), inc=dv)
            M["xp"][(c, h)] = dv.v
            p.wait("vector", ac, M["exp"][(c, h)])
            if h >= 1:
                p.wait("vector", pe, M["y"][(c, h - 1)])
            p.add("vector", lambda v, da=dam[(c * G_I + h) % 3]: v.tensor_tensor_scan(
                out=hm[:], data0=da[:], data1=xpm[:], initial=0.0,
                op0=ALU.mult, op1=ALU.add), inc=dv)
            M["scan"][(c, h)] = dv.v
            p.add("vector", lambda v, h=h: v.tensor_copy(
                out=carry.ap()[:, h * NST:(h + 1) * NST].unsqueeze(2),
                in_=hm3[:, :, TC2:TC2 + 1]))
            if h == 0:
                p.wait("vector", ac, M["bc_ev_all"]["c"])
            p.add("vector", lambda v: v.tensor_tensor(
                out=hm[:], in0=hm[:], in1=cm[:], op=ALU.mult), inc=dv)
            M["hc"][(c, h)] = dv.v
            if h >= 2:
                emit_gate(c, h - 2)

            p.wait("tensor", dv, M["hc"][(c, h)])
            pyt = ps_y[h % 2]
            if h >= 2:
                p.wait("tensor", dv, M["gate"][(c, h - 2)])
            else:
                p.wait("tensor", ac, M["bc_ev_all"]["c"])
            for n in range(NST):
                p.add("tensor", lambda t, n=n, pyt=pyt: t.matmul(
                    out=pyt[:, 0:TC2], lhsT=idn[:], rhs=hcm3[:, n, 1:TC2 + 1],
                    start=(n == 0), stop=False))
            p.add("tensor", lambda t, h=h, pyt=pyt, c=c: t.matmul(
                out=pyt[:, 0:TC2], lhsT=dd[:, h * 128:(h + 1) * 128],
                rhs=ub[c % 2][:, h * TC2:(h + 1) * TC2], start=False, stop=True),
                inc=pe)
            M["y"][(c, h)] = pe.v

        def emit_gate(c, h):
            p.wait("vector", ac, M["exp"][(c, h)])
            p.wait("vector", pe, M["y"][(c, h)])
            p.add("vector", lambda v, h=h, c=c: v.tensor_tensor(
                out=deltab[c % 2].ap()[:, h * TC2:(h + 1) * TC2],
                in0=ps_y[h % 2][:, 0:TC2],
                in1=sres[c % 2][:, h * TC2:(h + 1) * TC2], op=ALU.mult), inc=dv)
            M["gate"][(c, h)] = dv.v

        def emit_wout_load(c, rnd, s, seg_mm):
            buf = woutb[s % 2]
            if c == 0 and rnd == 0 and s <= 1:
                p.wait("sync", v_wcast[1], v_wcast[2])
            if (rnd, s) == (0, 0) and (c + 1, 7) in M["win_mm"]:
                p.wait("sync", pe, M["win_mm"][(c + 1, 7)])
            if (rnd, s) != (0, 0) or c >= 1:
                key = (rnd, s - 2) if s >= 2 else (rnd - 1, s + 2)
                if key in seg_mm:
                    p.wait("sync", pe, seg_mm[key])
                elif c >= 1:
                    p.wait("sync", pe, M["wout_mm"][c - 1])
            p.add("sync", lambda ss, rnd=rnd, s=s, buf=buf: ss.dma_start(
                out=buf[:], in_=wout_bf[rnd, 4 * s:4 * s + 4].rearrange(
                    "k p c -> p k c")), inc=ld, k=16)
            M["wo_ld"][(c, rnd, s)] = ld.v

        WO_POOL = [(ps_b[0], "b0"), (ps_b[1], "b1"), (ps_bc, "bc"), (ps_sq, "sq")]

        def emit_wout_seg(c, rnd, s, seg_mm):
            pool = WO_POOL
            buf = woutb[s % 2]
            if (c, rnd, s) not in M["wo_ld"]:
                emit_wout_load(c, rnd, s, seg_mm)
            p.wait("tensor", ld, M["wo_ld"][(c, rnd, s)])
            if rnd == 0:
                p.wait("tensor", dv, M["gate"][(c, 4 * s + 3)])
            if rnd == 0 and s == 0:
                p.wait("tensor", dv, M["lnbc"].get(c + 1, M["lnbc"][c]))
                p.wait("tensor", dv, M["sum_rd"].get(c + 1, M["sum_rd"][c]))
                if (c + 1, 7) in M["wdt_pm"]:
                    p.wait("tensor", ac, M["wdt_pm"][(c + 1, 6)])
                    p.wait("tensor", ac, M["wdt_pm"][(c + 1, 7)])
            elif rnd == 1 and s == 0:
                for o in range(4):
                    p.wait("tensor", dv, M["outadd"][(c, pool[o][1])])
            for o in range(4):
                pm = pool[o][0]
                for j in range(4):
                    kk = 4 * s + j
                    p.add("tensor", lambda t, o=o, j=j, kk=kk, buf=buf, pm=pm,
                          s=s: t.matmul(
                        out=pm[:, 0:TC2],
                        lhsT=buf[:, j * 512 + o * 128:j * 512 + (o + 1) * 128],
                        rhs=deltab[c % 2].ap()[:, kk * TC2:(kk + 1) * TC2],
                        start=(s == 0 and j == 0), stop=(s == 3 and j == 3)),
                        inc=(pe if j == 3 else None))
            seg_mm[(rnd, s)] = pe.v

        def emit_wout_tail(c, seg_mm):
            pool = WO_POOL
            for rnd in range(2):
                for s in range(4):
                    if (rnd, s) not in seg_mm:
                        emit_wout_seg(c, rnd, s, seg_mm)
                for o in range(4):
                    pm, pmname = pool[o]
                    oo = rnd * 4 + o
                    p.wait("vector", pe, seg_mm[(rnd, 3)] - (3 - o))
                    p.add("vector", lambda v, oo=oo, pm=pm, c=c: v.tensor_tensor(
                        out=xbuf[c % 2][:, oo * TC2:(oo + 1) * TC2],
                        in0=pm[:, 0:TC2],
                        in1=xbuf[c % 2][:, oo * TC2:(oo + 1) * TC2], op=ALU.add),
                        inc=dv)
                    M["outadd"][(c, pmname)] = dv.v
            M["wout_mm"][c] = pe.v
            v_out = dv.v
            p.wait("sync", dv, v_out)
            p.add("sync", lambda s, c=c: s.dma_start(
                out=xout_t[:, :, c * TC2:(c + 1) * TC2], in_=xbuf[c % 2][:]), inc=st, k=16)
            M["store"] = st.v

        SP_ACTS = {9: (0,), 10: (1, 2), 11: (3, 4), 12: (5, 6), 13: (7,)}
        SP_ADDS = {11: (0,), 12: (1, 2), 13: (3, 4), 14: (5, 6)}

        def pre_slots(c, h):
            if h == 0:
                pre_load(c)
            elif h == 1:
                pre_ln_a(c)
            elif h == 2:
                pre_ln_b(c)
                pre_halo(c)
            elif 3 <= h <= 6:
                pre_win(c, (h - 3) * 2)
                pre_win(c, (h - 3) * 2 + 1)
            elif h in (7, 8):
                pre_conv(c, range((h - 7) * 8, (h - 6) * 8))
            elif h == 9:
                pre_wx(c)
            if h in SP_ACTS:
                pre_sp_act(c, SP_ACTS[h])

        M["bc_ev"] = {}
        M["sp_ln"] = {}
        M["wo_ld"] = {}
        M["sp_add_h"] = {}
        M["bc_ev_all"] = {}
        M["halo"] = 0
        M["wdt_mm_last"] = 0
        M["bc_mm_last"] = 0
        M["wdt_last"] = 0

        for h in range(G_I):
            if h in SP_ADDS:
                pre_sp_add(0, SP_ADDS[h])
            pre_slots(0, h)
        pre_sp_add(0, (7,))
        pre_bcast(0, "b")
        pre_bcast(0, "c")
        for h0 in range(3):
            emit_du(0, h0)
            emit_dexp(0, h0)

        for c in range(NCH2):
            wo_seg = {}
            for h in range(G_I):
                if c + 1 < NCH2 and h in SP_ADDS:
                    pre_sp_add(c + 1, SP_ADDS[h])
                emit_scan_iter(c, h)
                if c + 1 < NCH2:
                    pre_slots(c + 1, h)
                if h in (10, 11):
                    emit_wout_load(c, 0, h - 10, {})
                if h + 3 < G_I:
                    emit_du(c, h + 3)
                    emit_dexp(c, h + 3)
                if h in (13, 14) and c + 1 < NCH2:
                    emit_du(c + 1, h - 13)
                    emit_dexp(c + 1, h - 13)
                if h == 14:
                    for s0 in range(3):
                        emit_wout_seg(c, 0, s0, wo_seg)
            if c + 1 < NCH2:
                pre_sp_add(c + 1, (7,))
            emit_gate(c, G_I - 2)
            emit_gate(c, G_I - 1)
            emit_wout_tail(c, wo_seg)
            if c + 1 < NCH2:
                pre_bcast(c + 1, "b")
                emit_du(c + 1, 2)
                emit_dexp(c + 1, 2)
                pre_bcast(c + 1, "c")

        p.wait("sync", st, st.v)



def build_program(emit=("p1", "p2", "p3")):
    nc = bass.Bass("TRN2", target_bir_lowering=False, debug=False)
    inp = {}

    def ein(name, shape):
        inp[name] = nc.dram_tensor(name, shape, F32, kind="ExternalInput").ap()

    ein("x0", [D_MODEL, T])
    ein("wrT", [D_MODEL, D_MODEL])
    ein("wkT", [D_MODEL, D_MODEL])
    ein("wvT", [D_MODEL, D_MODEL])
    ein("mu", [128, 16])
    ein("identity", [128, 128])
    ein("ones_c", [128, 1])
    ein("ones_r", [1, 128])
    ein("win_t", [8, 8, 128, 512])
    ein("wout_t", [2, 16, 128, 512])
    ein("wx", [D_IN, 96])
    ein("wdt", [DT_RANK, D_IN])
    ein("a_log", [128, G_I * NST])
    ein("convw", [128, 64])
    ein("convb", [128, G_I])
    ein("bdt", [128, G_I])
    ein("bdt_r", [1, D_IN])
    ein("dvec", [128, NST])
    ein("selb", [16, 16 * 128])

    out = nc.dram_tensor("out", [D_MODEL, T], F32, kind="ExternalOutput").ap()
    if _DBG:
        dbg = {}
        for nm, shape in (("xnx", [128, G_M * (TC1 + 1)]), ("xrb", [128, G_M * TC1]),
                          ("xkb", [128, G_M * TC1]), ("rb", [128, G_M * TC1]),
                          ("kab", [128, G_M * TC1])):
            dbg[nm] = nc.dram_tensor("dbg_" + nm, shape, F32, kind="ExternalOutput").ap()
        for nm, shape in (("a_sb", [128, G_I * NST]), ("deltab", [128, G_I * TC2]),
                          ("ub", [128, G_I * TC2]), ("sres", [128, G_I * TC2]),
                          ("carry", [128, G_I * NST]), ("dam", [128, FMRG]),
                          ("bm", [128, FMRG]), ("cm", [128, FMRG]),
                          ("dub", [128, G_I * TC2]), ("xceD", [128, G_I * (TC2 + 3)])):
            dbg[nm] = nc.dram_tensor("dbgm_" + nm, shape, F32, kind="ExternalOutput").ap()
        inp["_dbg"] = dbg
        dbs = {}
        for nm, shape, dt in (("s_dam", [128, FMRG], BF16), ("s_xpm", [128, FMRG], BF16),
                              ("s_hm", [128, FMRG], BF16), ("s_carry", [128, G_I * NST], F32),
                              ("s_dain", [128, FMRG], BF16)):
            dbs[nm] = nc.dram_tensor("dbs_" + nm, shape, dt, kind="ExternalOutput").ap()
        inp["_dbgs"] = dbs
    xkind = "ExternalOutput" if _DBG else "Internal"
    x1 = nc.dram_tensor("x1_scr", [D_MODEL, T], F32, kind=xkind).ap()
    x2 = nc.dram_tensor("x2_scr", [D_MODEL, T], F32, kind=xkind).ap()
    win_bf = nc.dram_tensor("win_bf", [8, 8, 128, 512], BF16, kind="Internal").ap()
    wout_bf = nc.dram_tensor("wout_bf", [2, 16, 128, 512], BF16, kind="Internal").ap()
    scr = {"win_bf": win_bf, "wout_bf": wout_bf}

    p = Prog()
    with ExitStack() as es:
        sems = {k: Ctr(es, nc, k) for k in ("ld", "lg", "pe", "dv", "ac", "st", "gp", "bar")}
        block = es.enter_context(nc.Block())

        chain = []
        if "p1" in emit:
            chain.append(("ffn", inp["x0"], x1 if ("p2" in emit or "p3" in emit) else out, "f1"))
        if "p2" in emit:
            src = x1 if "p1" in emit else inp["x0"]
            chain.append(("mamba", src, x2 if "p3" in emit else out, "m"))
        if "p3" in emit:
            src = x2 if "p2" in emit else (x1 if "p1" in emit else inp["x0"])
            chain.append(("ffn", src, out, "f3"))

        wc_stack = ExitStack()
        need_wcast = any(k == "mamba" for k, _, _, _ in chain)
        if need_wcast and chain[0][0] == "mamba":
            emit_wcast(nc, p, sems, inp, scr, wc_stack)
        for i, (kind, xi, xo, tag) in enumerate(chain):
            if i > 0:
                wc_stack.close()
                p.barrier(sems["bar"])
            if kind == "ffn":
                emit_ffn_pass(nc, p, sems, inp, xi, xo, tag)
            else:
                emit_mamba_pass(nc, p, sems, inp, xi, xo, scr)
            if i == 0 and need_wcast and kind == "ffn":
                emit_wcast(nc, p, sems, inp, scr, wc_stack)
        wc_stack.close()

        p.finalize(block)
    return nc



def _host_inputs(inputs):
    x = np.ascontiguousarray(np.transpose(np.asarray(inputs["L0_attn_x"]), (0, 2, 1)))
    f32 = np.float32

    def t2(a):
        return np.ascontiguousarray(np.asarray(a).T.astype(f32, copy=False))

    wrT = t2(inputs["L0_ffn_W_r"])
    wkT = t2(inputs["L0_ffn_W_k"])
    wvT = t2(inputs["L0_ffn_W_v"])

    mu = np.zeros((128, 16), f32)
    mu[:, 0:8] = np.asarray(inputs["L0_ffn_mu_r"]).reshape(8, 128).T
    mu[:, 8:16] = np.asarray(inputs["L0_ffn_mu_k"]).reshape(8, 128).T

    identity = np.eye(128, dtype=f32)
    ones_c = np.ones((128, 1), f32)
    ones_r = np.ones((1, 128), f32)

    win = np.asarray(inputs["L0_attn_Win"]).astype(f32, copy=False)
    win_t = np.zeros((8, 8, 128, 512), f32)
    for grp in range(8):
        for k in range(8):
            win_t[grp, k] = win[k * 128:(k + 1) * 128, grp * 512:(grp + 1) * 512]

    wout = np.asarray(inputs["L0_attn_Wout"]).astype(f32, copy=False)
    wout_t = np.zeros((2, 16, 128, 512), f32)
    for grp in range(2):
        for k in range(16):
            wout_t[grp, k] = wout[k * 128:(k + 1) * 128, grp * 512:(grp + 1) * 512]

    wx = np.ascontiguousarray(np.asarray(inputs["L0_attn_Wx"]).astype(f32, copy=False))
    wdt = np.ascontiguousarray(np.asarray(inputs["L0_attn_Wdt"]).astype(f32, copy=False))

    a_log = np.asarray(inputs["L0_attn_A_log"]).astype(f32, copy=False)
    a_log_t = np.zeros((128, G_I * NST), f32)
    for h in range(G_I):
        a_log_t[:, h * NST:(h + 1) * NST] = a_log[h * 128:(h + 1) * 128, :]

    convw = np.asarray(inputs["L0_attn_conv_w"]).astype(f32, copy=False).reshape(4, D_IN)
    convw_t = np.zeros((128, 64), f32)
    for h in range(G_I):
        for k in range(4):
            convw_t[:, h * 4 + k] = convw[k, h * 128:(h + 1) * 128]

    def cols(vec, g):
        return np.ascontiguousarray(np.asarray(vec).reshape(g, 128).T.astype(f32, copy=False))

    convb = cols(inputs["L0_attn_conv_b"], G_I)
    bdt = cols(inputs["L0_attn_bdt"], G_I)
    bdt_r = np.ascontiguousarray(np.asarray(inputs["L0_attn_bdt"]).astype(f32).reshape(1, D_IN))
    dvec = cols(inputs["L0_attn_D"], G_I)

    selb = np.zeros((16, 16 * 128), f32)
    for n in range(16):
        selb[n, n * 128:(n + 1) * 128] = 1.0

    shared = dict(wrT=wrT, wkT=wkT, wvT=wvT, mu=mu, identity=identity,
                  ones_c=ones_c, ones_r=ones_r, win_t=win_t, wout_t=wout_t,
                  wx=wx, wdt=wdt, a_log=a_log_t, convw=convw_t, convb=convb,
                  bdt=bdt, bdt_r=bdt_r, dvec=dvec, selb=selb)
    in_maps = [dict(shared, x0=np.ascontiguousarray(x[b])) for b in range(8)]
    return in_maps


def kernel(**inputs):
    in_maps = _host_inputs(inputs)
    nc = build_program(_EMIT)
    res = run_bass_kernel_spmd(nc, in_maps, core_ids=list(range(8)))
    outs = [r["out"] for r in res.results]
    y = np.stack(outs, axis=0)
    return np.ascontiguousarray(np.transpose(y, (0, 2, 1)).astype(np.float32))


# version 35 (final)
# speedup vs baseline: 1.0101x; 1.0101x over previous
import numpy as np
from contextlib import ExitStack
import concourse.bass as bass
import concourse.mybir as mybir
from concourse.bass_utils import run_bass_kernel_spmd

F32 = mybir.dt.float32
BF16 = mybir.dt.bfloat16
AF = mybir.ActivationFunctionType
ALU = mybir.AluOpType

D_MODEL = 1024
D_IN = 2048
NST = 16
DT_RANK = 64
T = 4096
G_M = D_MODEL // 128
G_I = D_IN // 128
EPS = 1e-5

TC1 = 512
NCH1 = T // TC1
TC2 = 256
NCH2 = T // TC2
SEG = TC2 + 1
FMRG = NST * SEG

_EMIT = ("p1", "p2", "p3")
_DBG = False


class Ctr:
    def __init__(self, es, nc, name):
        self.h = es.enter_context(nc.semaphore(name))
        self.v = 0


class Prog:
    ENGS = ("sync", "tensor", "vector", "scalar", "gpsimd")

    def __init__(self):
        self.q = {k: [] for k in self.ENGS}

    def add(self, eng, f, inc=None, k=1):
        if inc is not None:
            inc.v += k
            sem = inc.h
            self.q[eng].append(lambda e, f=f, sem=sem, k=k: f(e).then_inc(sem, k))
            return inc.v
        self.q[eng].append(f)
        return None

    def wait(self, eng, ctr, v=None):
        vv = ctr.v if v is None else v
        if vv <= 0:
            return
        self.q[eng].append(lambda e, s=ctr.h, vv=vv: e.wait_ge(s, vv))

    def barrier(self, bar):
        for eng in self.ENGS:
            self.add(eng, lambda e: e.nop(), inc=bar, k=1)
        tgt = bar.v
        for eng in self.ENGS:
            self.wait(eng, bar, tgt)

    def finalize(self, block):
        def run(e, fs):
            for f in fs:
                f(e)
        block.sync(lambda e: run(e, self.q["sync"]))
        block.tensor(lambda e: run(e, self.q["tensor"]))
        block.vector(lambda e: run(e, self.q["vector"]))
        block.scalar(lambda e: run(e, self.q["scalar"]))
        block.gpsimd(lambda e: run(e, self.q["gpsimd"]))



def emit_ffn_pass(nc, p, sems, inp, xin, xout, tag):
    ld, pe, dv, ac, st = sems["ld"], sems["pe"], sems["dv"], sems["ac"], sems["st"]
    lg = sems["lg"]
    xin_t = xin.rearrange("(g p) t -> p g t", p=128)
    xout_t = xout.rearrange("(g p) t -> p g t", p=128)

    with ExitStack() as les:
        sb = lambda name, shape, dt=F32: les.enter_context(
            nc.sbuf_tensor(tag + name, shape, dt))
        pt = lambda name, shape: les.enter_context(
            nc.psum_tensor(tag + name, shape, F32))

        wr = sb("wr", [128, G_M * D_MODEL], BF16)
        wk = sb("wk", [128, G_M * D_MODEL], BF16)
        wv = sb("wv", [128, G_M * D_MODEL], BF16)
        mxd = sb("mxd", [128, G_M * TC1], BF16)
        muq = sb("muq", [128, 32], F32)
        ones_c = sb("onc", [128, 1], F32)
        ones_cb = sb("oncb", [128, 1], BF16)
        ones_r = sb("onr", [1, 128], F32)
        idn = sb("idn", [128, 128], BF16)
        xbuf = [sb(f"xb{i}", [128, G_M * TC1], F32) for i in range(2)]
        xnx = [sb(f"xnx{i}", [128, G_M * (TC1 + 1)], BF16) for i in range(2)]
        xsq = sb("xsq", [128, G_M * TC1], BF16)
        xrb = sb("xrb", [128, G_M * TC1], BF16)
        xkb = sb("xkb", [128, G_M * TC1], BF16)
        rb = sb("rb", [128, G_M * TC1], BF16)
        kab = sb("kab", [128, G_M * TC1], BF16)
        rowm = sb("rowm", [1, TC1], F32)
        rowv = sb("rowv", [1, TC1], F32)
        rowv2 = sb("rowv2", [1, TC1], F32)
        tmp1 = sb("tmp1", [128, TC1], F32)

        ps_s = pt("ps_s", [1, TC1])
        ps_q = pt("ps_q", [1, TC1])
        ps_bc = pt("ps_bc", [128, 2 * TC1])
        ps_pr = [pt(f"ps_pr{i}", [128, TC1]) for i in range(4)]

        xnx3 = [x.ap().rearrange("p (g t) -> p g t", g=G_M) for x in xnx]

        p.add("gpsimd", lambda g: g.dma_start(
            out=wr[:], in_=inp["wrT"].rearrange("(g p) d -> p g d", p=128)), inc=lg, k=16)
        p.add("gpsimd", lambda g: g.dma_start(
            out=wk[:], in_=inp["wkT"].rearrange("(g p) d -> p g d", p=128)), inc=lg, k=16)
        p.add("gpsimd", lambda g: g.dma_start(
            out=wv[:], in_=inp["wvT"].rearrange("(g p) d -> p g d", p=128)), inc=lg, k=16)
        p.add("gpsimd", lambda g: g.dma_start(out=idn[:], in_=inp["identity"]), inc=lg, k=16)
        p.add("gpsimd", lambda g: g.dma_start(out=ones_cb[:], in_=inp["ones_c"]), inc=lg, k=16)
        p.add("sync", lambda s: s.dma_start(out=muq[:, 0:16], in_=inp["mu"]), inc=ld, k=16)
        p.add("sync", lambda s: s.dma_start(out=ones_c[:], in_=inp["ones_c"]), inc=ld, k=16)
        p.add("sync", lambda s: s.dma_start(out=ones_r[:], in_=inp["ones_r"]), inc=ld, k=16)
        cons = sb("cons", [128, 4], F32)
        v_ld0 = ld.v
        v_lg0 = lg.v

        p.wait("vector", ld, v_ld0)
        p.wait("vector", lg, v_lg0)
        p.add("vector", lambda v: v.tensor_scalar(
            out=muq[:, 16:32], in0=muq[:, 0:16], scalar1=-1.0, scalar2=1.0,
            op0=ALU.mult, op1=ALU.add))
        p.add("vector", lambda v: v.memset(cons[:, 2:3], EPS))
        p.add("vector", lambda v: v.memset(cons[:, 3:4], -0.5))
        p.add("vector", lambda v: v.memset(xnx3[0][:, :, 0:1], 0.0))
        p.add("vector", lambda v: v.memset(xnx3[1][:, :, 0:1], 0.0), inc=dv)
        v_prep = dv.v

        F = {"ld": {}, "sumq": {}, "rowrd": {}, "bcrd": {}, "apply": {},
             "mixmm": {}, "mixr": {}, "rkmm": {}, "vmm": {}, "gates": {},
             "store": {}, "ld_h": {},
             "mixev": {}, "rkev": {}, "sq": {}}

        def ffn_load(c):
            if c >= 2:
                p.wait("sync", st, F["store"][c - 2])
            p.add("sync", lambda s, c=c: s.dma_start(
                out=xbuf[c % 2][:, 0:4 * TC1],
                in_=xin_t[:, 0:4, c * TC1:(c + 1) * TC1]), inc=ld, k=16)
            F["ld_h"][c] = ld.v
            p.add("sync", lambda s, c=c: s.dma_start(
                out=xbuf[c % 2][:, 4 * TC1:8 * TC1],
                in_=xin_t[:, 4:8, c * TC1:(c + 1) * TC1]), inc=ld, k=16)
            F["ld"][c] = ld.v

        def ffn_ln(c):
            xb = xbuf[c % 2]
            xo3 = xnx3[c % 2]
            p.wait("tensor", ld, F["ld_h"][c])
            if c == 0:
                p.wait("tensor", dv, v_prep)
                p.wait("tensor", lg, v_lg0)
            else:
                p.wait("tensor", dv, F["rowrd"][c - 1])
            for g in range(G_M):
                if g == 4:
                    p.wait("tensor", ld, F["ld"][c])
                p.add("tensor", lambda t, g=g, xb=xb: t.matmul(
                    out=ps_s[:], lhsT=ones_c[:], rhs=xb[:, g * TC1:(g + 1) * TC1],
                    start=(g == 0), stop=(g == G_M - 1)),
                    inc=(pe if g == G_M - 1 else None))
            p.wait("scalar", ld, F["ld"][c])
            if c >= 1:
                p.wait("scalar", pe, F["sumq"][c - 1])
            for g in range(G_M):
                p.add("scalar", lambda s, g=g, xb=xb: s.activation(
                    out=xsq[:, g * TC1:(g + 1) * TC1],
                    in_=xb[:, g * TC1:(g + 1) * TC1], func=AF.Square),
                    inc=(ac if g == G_M - 1 else None))
            v_xsq = ac.v
            p.wait("tensor", ac, v_xsq)
            for g in range(G_M):
                p.add("tensor", lambda t, g=g: t.matmul(
                    out=ps_q[:], lhsT=ones_cb[:], rhs=xsq[:, g * TC1:(g + 1) * TC1],
                    start=(g == 0), stop=(g == G_M - 1)),
                    inc=(pe if g == G_M - 1 else None))
            F["sumq"][c] = pe.v
            p.wait("vector", pe, F["sumq"][c])
            p.add("vector", lambda v: v.tensor_scalar(
                out=rowm[:], in0=ps_s[:], scalar1=1.0 / D_MODEL, scalar2=None,
                op0=ALU.mult))
            p.add("vector", lambda v: v.tensor_scalar(
                out=rowv[:], in0=ps_q[:], scalar1=1.0 / D_MODEL, scalar2=None,
                op0=ALU.mult))
            p.add("vector", lambda v: v.scalar_tensor_tensor(
                out=tmp1[0:1, :], in0=rowm[:], scalar=1.0, in1=rowm[:],
                op0=ALU.mult, op1=ALU.mult))
            p.add("vector", lambda v: v.tensor_sub(
                out=rowv[:], in0=rowv[:], in1=tmp1[0:1, :]), inc=dv)
            F["rowrd"][c] = dv.v
            p.wait("scalar", dv, dv.v)
            p.add("scalar", lambda s: s.activation(
                out=rowv2[:], in_=rowv[:], func=AF.Ln, bias=cons[0:1, 2:3]))
            p.add("scalar", lambda s: s.activation(
                out=rowv2[:], in_=rowv2[:], func=AF.Exp, scale=cons[0:1, 3:4]), inc=ac)
            v_rstd = ac.v
            p.wait("tensor", ac, v_rstd)
            if c >= 1:
                p.wait("tensor", dv, F["bcrd"][c - 1])
            p.add("tensor", lambda t: t.matmul(
                out=ps_bc[:, 0:TC1], lhsT=ones_r[:], rhs=rowm[:], start=True, stop=True))
            p.add("tensor", lambda t: t.matmul(
                out=ps_bc[:, TC1:2 * TC1], lhsT=ones_r[:], rhs=rowv2[:],
                start=True, stop=True), inc=pe)
            v_bc = pe.v
            p.wait("vector", pe, v_bc)
            if c >= 1:
                p.wait("vector", dv, F["mixmm"].get(c - 2, 0))
                p.add("vector", lambda v, c=c: v.tensor_copy(
                    out=xo3[:, :, 0:1], in_=xnx3[(c - 1) % 2][:, :, TC1:TC1 + 1]))
            for g in range(G_M):
                p.add("vector", lambda v, g=g, xb=xb: v.tensor_tensor(
                    out=tmp1[:], in0=xb[:, g * TC1:(g + 1) * TC1],
                    in1=ps_bc[:, 0:TC1], op=ALU.subtract))
                p.add("vector", lambda v, g=g, xo3=xo3: v.tensor_tensor(
                    out=xo3[:, g, 1:TC1 + 1], in0=tmp1[:],
                    in1=ps_bc[:, TC1:2 * TC1], op=ALU.mult),
                    inc=(dv if g == G_M - 1 else None))
            F["apply"][c] = dv.v
            F["bcrd"][c] = dv.v

        def ffn_mix(c):
            xo3 = xnx3[c % 2]
            p.wait("vector", dv, F["apply"][c])
            if c >= 1:
                p.wait("vector", pe, F["rkmm"][c - 1])
            for g in range(G_M):
                cur = xo3[:, g, 1:TC1 + 1]
                prv = xo3[:, g, 0:TC1]
                p.add("vector", lambda v, g=g, cur=cur, prv=prv: v.tensor_sub(
                    out=mxd[:, g * TC1:(g + 1) * TC1], in0=cur, in1=prv))
                p.add("vector", lambda v, g=g, prv=prv: v.scalar_tensor_tensor(
                    out=xrb[:, g * TC1:(g + 1) * TC1],
                    in0=mxd[:, g * TC1:(g + 1) * TC1],
                    scalar=muq[:, g:g + 1], in1=prv, op0=ALU.mult, op1=ALU.add),
                    inc=(dv if g == G_M - 1 else None))
            F["mixr"][c] = dv.v
            for g in range(G_M):
                prv = xo3[:, g, 0:TC1]
                p.add("vector", lambda v, g=g, prv=prv: v.scalar_tensor_tensor(
                    out=xkb[:, g * TC1:(g + 1) * TC1],
                    in0=mxd[:, g * TC1:(g + 1) * TC1],
                    scalar=muq[:, 8 + g:9 + g], in1=prv, op0=ALU.mult,
                    op1=ALU.add), inc=(dv if g == G_M - 1 else None))
            F["mixmm"][c] = dv.v
            F["mixev"][c] = dv.v

        def ffn_rk(c):
            pr_ev = {i: 0 for i in range(4)}
            for (wmat, src, isr) in ((wr, xrb, True), (wk, xkb, False)):
                p.wait("tensor", dv, F["mixr"][c] if isr else F["mixev"][c])
                for o in range(G_M):
                    pm = ps_pr[o % 4]
                    if pr_ev[o % 4]:
                        p.wait("tensor", ac, pr_ev[o % 4])
                    elif c >= 1:
                        p.wait("tensor", dv, F["gates"][c - 1])
                    for g in range(G_M):
                        p.add("tensor", lambda t, wmat=wmat, src=src, g=g, o=o, pm=pm: t.matmul(
                            out=pm[:],
                            lhsT=wmat[:, g * D_MODEL + o * 128:g * D_MODEL + (o + 1) * 128],
                            rhs=src[:, g * TC1:(g + 1) * TC1],
                            start=(g == 0), stop=(g == G_M - 1)),
                            inc=(pe if g == G_M - 1 else None))
                    vv = pe.v
                    p.wait("scalar", pe, vv)
                    if isr and o == 0 and c >= 1:
                        p.wait("scalar", dv, F["gates"][c - 1])
                    if (not isr) and o == 0 and c >= 1:
                        p.wait("scalar", pe, F["vmm"][c - 1])
                    fn = AF.Sigmoid if isr else AF.Relu
                    dstb = rb if isr else kab
                    p.add("scalar", lambda s, o=o, pm=pm, fn=fn, dstb=dstb: s.activation(
                        out=dstb[:, o * TC1:(o + 1) * TC1], in_=pm[:], func=fn), inc=ac)
                    pr_ev[o % 4] = ac.v
                    if not isr:
                        p.wait("vector", ac, ac.v)
                        p.add("vector", lambda v, o=o: v.tensor_tensor(
                            out=kab[:, o * TC1:(o + 1) * TC1],
                            in0=kab[:, o * TC1:(o + 1) * TC1],
                            in1=kab[:, o * TC1:(o + 1) * TC1], op=ALU.mult),
                            inc=(dv if o == G_M - 1 else None))
            F["rkmm"][c] = pe.v
            F["rkev"] = pr_ev
            F["sq"][c] = dv.v

        def ffn_v(c):
            v_ka = F["sq"][c]
            p.wait("tensor", dv, v_ka)
            pr_ev = dict(F["rkev"])
            pr_dv = {i: 0 for i in range(4)}
            xb = xbuf[c % 2]
            for o in range(G_M):
                pm = ps_pr[o % 4]
                if pr_ev[o % 4]:
                    p.wait("tensor", ac, pr_ev[o % 4])
                    pr_ev[o % 4] = 0
                if pr_dv[o % 4]:
                    p.wait("tensor", dv, pr_dv[o % 4])
                for g in range(G_M):
                    p.add("tensor", lambda t, g=g, o=o, pm=pm: t.matmul(
                        out=pm[:],
                        lhsT=wv[:, g * D_MODEL + o * 128:g * D_MODEL + (o + 1) * 128],
                        rhs=kab[:, g * TC1:(g + 1) * TC1],
                        start=(g == 0), stop=(g == G_M - 1)),
                        inc=(pe if g == G_M - 1 else None))
                vv = pe.v
                p.wait("vector", pe, vv)
                p.add("vector", lambda v, o=o, pm=pm: v.scalar_tensor_tensor(
                    out=tmp1[:], in0=rb[:, o * TC1:(o + 1) * TC1], scalar=0.5,
                    in1=pm[:], op0=ALU.mult, op1=ALU.mult))
                p.add("vector", lambda v, o=o, xb=xb: v.tensor_tensor(
                    out=xb[:, o * TC1:(o + 1) * TC1], in0=tmp1[:],
                    in1=xb[:, o * TC1:(o + 1) * TC1], op=ALU.add), inc=dv)
                pr_dv[o % 4] = dv.v
            F["vmm"][c] = pe.v
            F["gates"][c] = dv.v
            p.wait("sync", dv, F["gates"][c])
            p.add("sync", lambda s, c=c, xb=xb: s.dma_start(
                out=xout_t[:, :, c * TC1:(c + 1) * TC1], in_=xb[:]), inc=st, k=16)
            F["store"][c] = st.v

        ffn_load(0)
        ffn_ln(0)
        for c in range(NCH1):
            if c + 1 < NCH1:
                ffn_load(c + 1)
            ffn_mix(c)
            ffn_rk(c)
            if c + 1 < NCH1:
                ffn_ln(c + 1)
            ffn_v(c)

        p.wait("sync", st, st.v)


def emit_wcast(nc, p, sems, inp, scr, les):
    gp = sems["gp"]
    win_bf, wout_bf = scr["win_bf"], scr["wout_bf"]
    wint = inp["win_t"]
    woutt = inp["wout_t"]
    for grp in range(8):
        p.add("gpsimd", lambda g, grp=grp: g.dma_start(
            out=win_bf[grp], in_=wint[grp]), inc=gp, k=16)
    for grp in range(2):
        p.add("gpsimd", lambda g, grp=grp: g.dma_start(
            out=wout_bf[grp], in_=woutt[grp]), inc=gp, k=16)
    scr["v_wcast"] = ("gp", sems["gp"], sems["gp"].v)


def emit_mamba_pass(nc, p, sems, inp, xin, xout, scr):
    ld, pe, dv, ac, st, gp = (sems["ld"], sems["pe"], sems["dv"], sems["ac"],
                              sems["st"], sems["gp"])
    lg = sems["lg"]
    xin_t = xin.rearrange("(g p) t -> p g t", p=128)
    xout_t = xout.rearrange("(g p) t -> p g t", p=128)
    win_bf, wout_bf = scr["win_bf"], scr["wout_bf"]

    with ExitStack() as les:
        sb = lambda name, shape, dt=F32: les.enter_context(nc.sbuf_tensor("m" + name, shape, dt))
        pt = lambda name, shape: les.enter_context(nc.psum_tensor("m" + name, shape, F32))

        wx = sb("wx", [128, G_I * 96], BF16)
        wdt = sb("wdt", [64, D_IN], BF16)
        a_sb = sb("a", [128, G_I * NST], F32)
        convd = sb("convd", [128, 4 * G_I * 128], BF16)
        dd = sb("dd", [128, G_I * 128], BF16)
        convb = sb("convb", [128, G_I], F32)
        selbig = sb("selb", [33, 16 * 128 + TC2], BF16)
        selb = selbig.ap()[0:16, 0:16 * 128]
        idn = sb("idn", [128, 128], BF16)
        ones_c = sb("onc", [128, 1], F32)
        ones_cb = sb("oncb", [128, 1], BF16)
        ones_r = sb("onr", [1, 128], F32)
        cons = sb("cons2", [128, 4], F32)
        bdt_r = selbig.ap()[32:33, 0:D_IN]
        ones_rb = selbig.ap()[32:33, D_IN:D_IN + TC2]

        winb = sb("win", [128, 8 * 512], BF16)
        woutb1 = sb("wo1", [128, 4 * 512], BF16)

        xbuf = [sb(f"xb{i}", [128, G_M * TC2], F32) for i in range(2)]
        xnb = sb("xnb", [128, G_M * TC2], BF16)
        woutb = [xnb, woutb1]
        xsq = xnb
        zscr = sb("zscr", [128, 1536], F32)
        zscr2 = sb("zscr2", [128, 512], BF16)
        cvec = zscr.ap()[:, 256:320]
        dvec = zscr.ap()[:, 320:336]
        xce = sb("xce", [128, G_I * (TC2 + 3)], BF16)
        sres = [sb(f"sres{i}", [128, G_I * TC2], BF16) for i in range(2)]
        ub = [sb(f"ub{i}", [128, G_I * TC2], BF16) for i in range(2)]
        deltab = [sb(f"delta{i}", [128, G_I * TC2], BF16) for i in range(2)]
        dub = sb("dub", [128, 4 * TC2], BF16)
        dltb = sb("dltb", [64, TC2], BF16)
        bcsrc = sb("bcsrc", [16, 2 * TC2], BF16)
        bm = sb("bm", [128, FMRG], BF16)
        cm = sb("cm", [128, FMRG], BF16)
        dam2 = sb("dam2", [128, 3 * FMRG], BF16)
        dam = [dam2.ap()[:, i * FMRG:(i + 1) * FMRG] for i in range(3)]
        xpm = sb("xpm", [128, FMRG], BF16)
        hm = sb("hm", [128, FMRG], BF16)
        hcm = hm
        carry = sb("carry", [128, G_I * NST], BF16)
        tmp1 = zscr.ap()[:, 1024:1024 + TC2]

        rowm = zscr.ap()[0:1, 0:TC2]
        rowv = zscr.ap()[0:1, TC2:2 * TC2]
        rowv2 = zscr.ap()[0:1, 2 * TC2:3 * TC2]

        ps_sq = pt("ps_sq", [128, 512])
        ps_bc = pt("ps_bc", [128, 512])
        ps_a = [pt(f"ps_a{i}", [128, 512]) for i in range(2)]
        ps_b = [pt(f"ps_b{i}", [128, 512]) for i in range(2)]
        ps_y = [pt(f"ps_y{i}", [128, 512]) for i in range(2)]

        xce3 = xce.ap().rearrange("p (g t) -> p g t", g=G_I)
        bm3 = bm.ap().rearrange("p (n t) -> p n t", n=NST)
        cm3 = cm.ap().rearrange("p (n t) -> p n t", n=NST)
        dam3 = [x.rearrange("p (n t) -> p n t", n=NST) for x in dam]
        xpm3 = xpm.ap().rearrange("p (n t) -> p n t", n=NST)
        hm3 = hm.ap().rearrange("p (n t) -> p n t", n=NST)
        hcm3 = hm3

        p.add("gpsimd", lambda g: g.dma_start(
            out=wx[:], in_=inp["wx"].rearrange("(h p) c -> p h c", p=128)), inc=lg, k=16)
        p.add("gpsimd", lambda g: g.dma_start(out=wdt[:], in_=inp["wdt"]), inc=lg, k=16)
        p.add("gpsimd", lambda g: g.dma_start(out=selb, in_=inp["selb"]), inc=lg, k=16)
        p.add("gpsimd", lambda g: g.dma_start(out=idn[:], in_=inp["identity"]), inc=lg, k=16)
        p.add("gpsimd", lambda g: g.dma_start(out=ones_cb[:], in_=inp["ones_c"]), inc=lg, k=16)
        p.add("gpsimd", lambda g: g.dma_start(out=bdt_r, in_=inp["bdt_r"]), inc=lg, k=16)
        p.add("sync", lambda s: s.dma_start(out=zscr[:, 0:G_I * NST], in_=inp["a_log"]), inc=ld, k=16)
        p.add("sync", lambda s: s.dma_start(out=cvec, in_=inp["convw"]), inc=ld, k=16)
        p.add("sync", lambda s: s.dma_start(out=convb[:], in_=inp["convb"]), inc=ld, k=16)
        p.add("sync", lambda s: s.dma_start(out=dvec, in_=inp["dvec"]), inc=ld, k=16)
        p.add("sync", lambda s: s.dma_start(out=ones_c[:], in_=inp["ones_c"]), inc=ld, k=16)
        p.add("sync", lambda s: s.dma_start(out=ones_r[:], in_=inp["ones_r"]), inc=ld, k=16)
        v_ldp = ld.v
        v_lgp = lg.v

        p.wait("scalar", ld, v_ldp)
        p.add("scalar", lambda s: s.activation(
            out=a_sb[:], in_=zscr[:, 0:G_I * NST], func=AF.Exp), inc=ac)
        p.wait("vector", ac, ac.v)
        p.add("vector", lambda v: v.tensor_scalar(
            out=a_sb[:], in0=a_sb[:], scalar1=-1.0, scalar2=None, op0=ALU.mult))
        p.wait("vector", ld, v_ldp)
        p.wait("vector", lg, v_lgp)
        for tap in range(4):
            for h in range(G_I):
                dst = convd[:, (tap * G_I + h) * 128:(tap * G_I + h + 1) * 128]
                p.add("vector", lambda v, dst=dst, tap=tap, h=h: v.tensor_scalar(
                    out=dst, in0=idn[:], scalar1=cvec[:, h * 4 + tap:h * 4 + tap + 1],
                    scalar2=None, op0=ALU.mult))
        for h in range(G_I):
            dst = dd[:, h * 128:(h + 1) * 128]
            p.add("vector", lambda v, dst=dst, h=h: v.tensor_scalar(
                out=dst, in0=idn[:], scalar1=dvec[:, h:h + 1], scalar2=None,
                op0=ALU.mult))
        p.add("vector", lambda v: v.memset(ones_rb, 1.0))
        p.add("vector", lambda v: v.memset(cons[:, 0:1], -1.0))
        p.add("vector", lambda v: v.memset(cons[:, 1:2], 1.0))
        p.add("vector", lambda v: v.memset(cons[:, 2:3], EPS))
        p.add("vector", lambda v: v.memset(cons[:, 3:4], -0.5))
        p.add("vector", lambda v: v.memset(carry[:], 0.0))
        p.add("vector", lambda v: v.memset(xce3[:, :, 0:3], 0.0))
        p.add("vector", lambda v: v.memset(dam3[0][:, :, 0:1], 0.0))
        p.add("vector", lambda v: v.memset(dam3[1][:, :, 0:1], 0.0))
        p.add("vector", lambda v: v.memset(dam3[2][:, :, 0:1], 0.0), inc=dv)
        v_prep = dv.v

        v_wcast = scr["v_wcast"]

        M = {
            "store": 0,
            "xload": {},
            "apply": {},
            "win_mm": {},
            "win_half": {},
            "win_ev": {},
            "conv_ev": {},
            "wx_ev": 0,
            "bc_ev": {},
            "sp_add": {},
            "wdt_pm": {},
            "du": {},
            "ts": {},
            "exp": {},
            "xp": {},
            "scan": {},
            "hc": {},
            "y": {},
            "gate": {},
            "wout_mm": {},
            "outadd": {},
            "lnbc": {},
            "sum_rd": {},
        }

        def pre_load(c):
            if c >= 2:
                p.wait("sync", st, M["store"])
            p.add("sync", lambda s, c=c: s.dma_start(
                out=xbuf[c % 2][:], in_=xin_t[:, :, c * TC2:(c + 1) * TC2]), inc=ld, k=16)
            M["xload"][c] = ld.v

        def pre_ln_a(c):
            xb = xbuf[c % 2]
            p.wait("tensor", ld, M["xload"][c])
            if c == 0:
                p.wait("tensor", dv, v_prep)
                p.wait("tensor", lg, v_lgp)
            if c - 1 in M["sum_rd"]:
                p.wait("tensor", dv, M["sum_rd"][c - 1])
            p.wait("tensor", dv, M["outadd"].get((c - 2, "sq"), 0))
            for g in range(G_M):
                p.add("tensor", lambda t, g=g, xb=xb: t.matmul(
                    out=ps_sq[0:1, 0:TC2], lhsT=ones_c[:],
                    rhs=xb[:, g * TC2:(g + 1) * TC2],
                    start=(g == 0), stop=(g == G_M - 1)),
                    inc=(pe if g == G_M - 1 else None))
            p.wait("vector", ld, M["xload"][c])
            if c >= 1:
                p.wait("vector", pe, M["win_mm"][(c - 1, 7)])
            p.add("vector", lambda v, xb=xb: v.tensor_tensor(
                out=xsq[:], in0=xb[:], in1=xb[:], op=ALU.mult), inc=dv)
            v_sq = dv.v
            p.wait("tensor", dv, v_sq)
            for g in range(G_M):
                p.add("tensor", lambda t, g=g: t.matmul(
                    out=ps_sq[0:1, TC2:2 * TC2], lhsT=ones_cb[:],
                    rhs=xsq[:, g * TC2:(g + 1) * TC2],
                    start=(g == 0), stop=(g == G_M - 1)),
                    inc=(pe if g == G_M - 1 else None))
            M["lnsum"] = pe.v
            pre_ln_stats(c)

        def pre_ln_stats(c):
            p.wait("vector", pe, M["lnsum"])
            p.add("vector", lambda v: v.tensor_scalar(
                out=rowm, in0=ps_sq[0:1, 0:TC2], scalar1=1.0 / D_MODEL,
                scalar2=None, op0=ALU.mult))
            p.add("vector", lambda v: v.tensor_scalar(
                out=rowv, in0=ps_sq[0:1, TC2:2 * TC2], scalar1=1.0 / D_MODEL,
                scalar2=None, op0=ALU.mult))
            p.add("vector", lambda v: v.scalar_tensor_tensor(
                out=tmp1[0:1, :], in0=rowm, scalar=1.0, in1=rowm,
                op0=ALU.mult, op1=ALU.mult))
            p.add("vector", lambda v: v.tensor_sub(
                out=rowv, in0=rowv, in1=tmp1[0:1, :]), inc=dv)
            M["sum_rd"][c] = dv.v
            p.wait("scalar", dv, dv.v)
            p.add("scalar", lambda s: s.activation(
                out=rowv2, in_=rowv, func=AF.Ln, bias=cons[0:1, 2:3]))
            p.add("scalar", lambda s: s.activation(
                out=rowv2, in_=rowv2, func=AF.Exp, scale=cons[0:1, 3:4]), inc=ac)
            v_rstd = ac.v
            p.wait("tensor", ac, v_rstd)
            if c >= 2:
                p.wait("tensor", dv, M["outadd"][(c - 2, "bc")])
            p.add("tensor", lambda t: t.matmul(
                out=ps_bc[:, 0:TC2], lhsT=ones_r[:], rhs=rowm, start=True, stop=True))
            p.add("tensor", lambda t: t.matmul(
                out=ps_bc[:, TC2:2 * TC2], lhsT=ones_r[:], rhs=rowv2,
                start=True, stop=True), inc=pe)
            M["lnv_bcm"] = pe.v

        def pre_ln_b(c):
            xb = xbuf[c % 2]
            p.wait("vector", pe, M["lnv_bcm"])
            p.wait("vector", pe, M["wout_mm"].get(c - 2, 0))
            for g in range(G_M):
                p.add("vector", lambda v, g=g, xb=xb: v.tensor_tensor(
                    out=tmp1[:], in0=xb[:, g * TC2:(g + 1) * TC2],
                    in1=ps_bc[:, 0:TC2], op=ALU.subtract))
                p.add("vector", lambda v, g=g: v.tensor_tensor(
                    out=xnb[:, g * TC2:(g + 1) * TC2], in0=tmp1,
                    in1=ps_bc[:, TC2:2 * TC2], op=ALU.mult),
                    inc=(dv if g == G_M - 1 else None))
            M["apply"][c] = dv.v
            M["lnbc"][c] = dv.v

        def pre_win(c, grp):
            if c == 0 and grp == 0:
                p.wait("sync", v_wcast[1], v_wcast[2])
            if (c, grp - 1, 0) in M["win_half"]:
                p.wait("sync", pe, M["win_half"][(c, grp - 1, 0)])
            elif c >= 1:
                p.wait("sync", pe, M["win_mm"][(c - 1, 7)])
            p.add("sync", lambda s, grp=grp: s.dma_start(
                out=winb[:, 0:2048],
                in_=win_bf[grp, 0:4].rearrange("k p c -> p k c")), inc=ld, k=16)
            v_wa = ld.v
            if (c, grp - 1) in M["win_mm"]:
                p.wait("sync", pe, M["win_mm"][(c, grp - 1)])
            elif c >= 1:
                p.wait("sync", pe, M["win_mm"][(c - 1, 7)])
            p.add("sync", lambda s, grp=grp: s.dma_start(
                out=winb[:, 2048:4096],
                in_=win_bf[grp, 4:8].rearrange("k p c -> p k c")), inc=ld, k=16)
            v_wb = ld.v
            p.wait("tensor", ld, v_wa)
            if grp == 0:
                p.wait("tensor", dv, M["apply"][c])
                p.wait("scalar", dv, M["halo"])
            for o in range(4):
                oo = grp * 4 + o
                pm = [ps_a[0], ps_a[1], ps_b[0], ps_b[1]][oo % 4]
                if (c, oo - 4) in M["win_ev"]:
                    p.wait("tensor", ac, M["win_ev"][(c, oo - 4)])
                elif c >= 1 and oo < 2:
                    p.wait("tensor", ac, M["bc_last"])
                elif c >= 2 and oo in (2, 3):
                    p.wait("tensor", dv, M["outadd"][(c - 2, "b%d" % (oo - 2))])
                for k in range(4):
                    p.add("tensor", lambda t, k=k, o=o, pm=pm: t.matmul(
                        out=pm[:, 0:TC2],
                        lhsT=winb[:, k * 512 + o * 128:k * 512 + (o + 1) * 128],
                        rhs=xnb[:, k * TC2:(k + 1) * TC2],
                        start=(k == 0), stop=False),
                        inc=(pe if o == 3 and k == 3 else None))
            M["win_half"][(c, grp, 0)] = pe.v
            p.wait("tensor", ld, v_wb)
            for o in range(4):
                oo = grp * 4 + o
                pm = [ps_a[0], ps_a[1], ps_b[0], ps_b[1]][oo % 4]
                for k in range(4, 8):
                    p.add("tensor", lambda t, k=k, o=o, pm=pm: t.matmul(
                        out=pm[:, 0:TC2],
                        lhsT=winb[:, k * 512 + o * 128:k * 512 + (o + 1) * 128],
                        rhs=xnb[:, k * TC2:(k + 1) * TC2],
                        start=False, stop=(k == 7)),
                        inc=(pe if k == 7 else None))
                vv = pe.v
                p.wait("scalar", pe, vv)
                if oo < 16:
                    p.add("scalar", lambda s, oo=oo, pm=pm: s.copy(
                        out=xce3[:, oo, 3:TC2 + 3], in_=pm[:, 0:TC2]), inc=ac)
                else:
                    p.add("scalar", lambda s, oo=oo, pm=pm, c=c: s.activation(
                        out=sres[c % 2][:, (oo - 16) * TC2:(oo - 15) * TC2],
                        in_=pm[:, 0:TC2], func=AF.Silu), inc=ac)
                M["win_ev"][(c, oo)] = ac.v
            M["win_mm"][(c, grp)] = pe.v

        def pre_conv(c, hs):
            for h in hs:
                pm = ps_b[h % 2]
                if (c, h - 2) in M["conv_ev"]:
                    p.wait("tensor", ac, M["conv_ev"][(c, h - 2)])
                elif h < 2:
                    p.wait("tensor", ac, M["win_ev"][(c, 30 + h)])
                if h == 0:
                    p.wait("tensor", ac, M["win_ev"][(c, 15)])
                for tap in range(4):
                    p.add("tensor", lambda t, h=h, tap=tap, pm=pm: t.matmul(
                        out=pm[:, 0:TC2],
                        lhsT=convd[:, (tap * G_I + h) * 128:(tap * G_I + h + 1) * 128],
                        rhs=xce3[:, h, tap:tap + TC2],
                        start=(tap == 0), stop=(tap == 3)),
                        inc=(pe if tap == 3 else None))
                vv = pe.v
                p.wait("scalar", pe, vv)
                if c >= 2:
                    p.wait("scalar", pe, M["y"][(c - 2, h)])
                p.add("scalar", lambda s, h=h, pm=pm, c=c: s.activation(
                    out=ub[c % 2][:, h * TC2:(h + 1) * TC2], in_=pm[:, 0:TC2],
                    func=AF.Silu, bias=convb[:, h:h + 1]), inc=ac)
                M["conv_ev"][(c, h)] = ac.v

        def pre_wx(c):
            p.wait("tensor", ac, M["conv_ev"][(c, 15)])
            p.wait("tensor", ac, M["win_ev"][(c, 28)])
            p.wait("tensor", ac, M["win_ev"][(c, 29)])
            if c >= 1:
                p.wait("scalar", pe, M["wdt_mm_last"])
                p.wait("scalar", pe, M["bc_mm_last"])
            u = ub[c % 2]
            for k in range(G_I):
                p.add("tensor", lambda t, k=k, u=u: t.matmul(
                    out=ps_a[0][0:80, 0:TC2], lhsT=wx[:, k * 96:k * 96 + 80],
                    rhs=u[:, k * TC2:(k + 1) * TC2],
                    start=(k == 0), stop=(k == G_I - 1)),
                    inc=(pe if k == G_I - 1 else None))
            for k in range(G_I):
                p.add("tensor", lambda t, k=k, u=u: t.matmul(
                    out=ps_a[1][0:16, 0:TC2], lhsT=wx[:, k * 96 + 80:(k + 1) * 96],
                    rhs=u[:, k * TC2:(k + 1) * TC2],
                    start=(k == 0), stop=(k == G_I - 1)),
                    inc=(pe if k == G_I - 1 else None))
            v_xdbl = pe.v
            p.wait("scalar", pe, v_xdbl)
            p.add("scalar", lambda s: s.copy(out=dltb[:], in_=ps_a[0][0:64, 0:TC2]))
            p.add("scalar", lambda s: s.copy(
                out=bcsrc[:, 0:TC2], in_=ps_a[0][64:80, 0:TC2]))
            p.add("scalar", lambda s: s.copy(
                out=bcsrc[:, TC2:2 * TC2], in_=ps_a[1][0:16, 0:TC2]), inc=ac)
            M["wx_ev"] = ac.v

        def pre_bcast(c, which):
            wi, dst3 = (0, bm3) if which == "b" else (1, cm3)
            p.wait("tensor", ac, M["wx_ev"])
            if c >= 1:
                if which == "b":
                    p.wait("scalar", dv, M["xp"][(c - 1, 15)])
                else:
                    p.wait("scalar", dv, M["hc"][(c - 1, 15)])
            for n in range(NST):
                pm = [ps_a[0], ps_a[1], ps_y[0], ps_y[1]][n % 4]
                if (n % 4) in M["bc_ev"]:
                    p.wait("tensor", ac, M["bc_ev"][n % 4])
                elif n % 4 >= 2 and c >= 1:
                    p.wait("tensor", dv, M["gate"][(c - 1, G_I - 4 + n)])
                p.add("tensor", lambda t, n=n, wi=wi, pm=pm: t.matmul(
                    out=pm[:, 0:TC2], lhsT=selb[:, n * 128:(n + 1) * 128],
                    rhs=bcsrc[:, wi * TC2:(wi + 1) * TC2], start=True, stop=True), inc=pe)
                vv = pe.v
                p.wait("scalar", pe, vv)
                p.add("scalar", lambda s, n=n, dst3=dst3, pm=pm: s.copy(
                    out=dst3[:, n, 1:TC2 + 1], in_=pm[:, 0:TC2]), inc=ac)
                M["bc_ev"][n % 4] = ac.v
            M["bc_mm_last"] = pe.v
            M["bc_ev_all"][which] = ac.v
            if which == "c":
                M["bc_last"] = ac.v
                M["bc_ev"] = {}

        def _sp_regions(pi):
            if pi % 2 == 0:
                return zscr.ap()[:, 0:512], zscr.ap()[:, 512:1024]
            return zscr.ap()[:, 1024:1536], zscr2.ap()[:, 0:512]

        def pre_sp_act(c, pairs):
            for pi in pairs:
                h0 = 2 * pi
                pm = ps_b[pi % 2]
                rr, aa = _sp_regions(pi)
                p.wait("tensor", ac, M["wx_ev"])
                if (c, pi - 2) in M["wdt_pm"]:
                    p.wait("tensor", ac, M["wdt_pm"][(c, pi - 2)])
                else:
                    p.wait("tensor", ac, M["conv_ev"][(c, 14 + (pi % 2))])
                for j in range(2):
                    h = h0 + j
                    p.add("tensor", lambda t, h=h, j=j, pm=pm: t.matmul(
                        out=pm[:, j * TC2:(j + 1) * TC2],
                        lhsT=wdt[:, h * 128:(h + 1) * 128], rhs=dltb[:],
                        start=(j == 0), stop=False))
                for j in range(2):
                    h = h0 + j
                    p.add("tensor", lambda t, h=h, j=j, pm=pm: t.matmul(
                        out=pm[:, j * TC2:(j + 1) * TC2],
                        lhsT=bdt_r[0:1, h * 128:(h + 1) * 128], rhs=ones_rb,
                        start=False, stop=(j == 1)),
                        inc=(pe if j == 1 else None))
                vv = pe.v
                M["wdt_mm_last"] = pe.v
                p.wait("scalar", pe, vv)
                if (c, pi - 2) in M["sp_add"]:
                    p.wait("scalar", dv, M["sp_add"][(c, pi - 2)])
                p.add("scalar", lambda s, pm=pm, rr=rr: s.activation(
                    out=rr, in_=pm[:, 0:512], func=AF.Relu))
                p.add("scalar", lambda s, pm=pm, aa=aa: s.activation(
                    out=aa, in_=pm[:, 0:512], func=AF.Abs), inc=ac)
                M["wdt_pm"][(c, pi)] = ac.v
                p.add("scalar", lambda s, aa=aa: s.activation(
                    out=aa, in_=aa, func=AF.Exp, scale=cons[:, 0:1]))
                p.add("scalar", lambda s, aa=aa: s.activation(
                    out=aa, in_=aa, func=AF.Ln, bias=cons[:, 1:2]), inc=ac)
                M["sp_ln"][(c, pi)] = ac.v
            M["wdt_last"] = ac.v

        def pre_sp_add(c, pairs):
            for pi in pairs:
                h0 = 2 * pi
                rr, aa = _sp_regions(pi)
                p.wait("vector", ac, M["sp_ln"][(c, pi)])
                p.add("vector", lambda v, h0=h0, rr=rr, aa=aa: v.tensor_tensor(
                    out=deltab[c % 2][:, h0 * TC2:(h0 + 2) * TC2],
                    in0=aa, in1=rr, op=ALU.add), inc=dv)
                M["sp_add"][(c, pi)] = dv.v
                M["sp_add_h"][(c, h0)] = dv.v
                M["sp_add_h"][(c, h0 + 1)] = dv.v

        def pre_halo(c):
            if c > 0:
                p.add("vector", lambda v: v.tensor_copy(
                    out=xce3[:, :, 0:3], in_=xce3[:, :, TC2:TC2 + 3]), inc=dv)
            M["halo"] = dv.v

        def emit_du(c, h):
            p.wait("vector", dv, M["sp_add_h"][(c, h)])
            p.add("vector", lambda v, h=h, c=c: v.tensor_tensor(
                out=dub.ap()[:, (h % 4) * TC2:(h % 4 + 1) * TC2],
                in0=deltab[c % 2][:, h * TC2:(h + 1) * TC2],
                in1=ub[c % 2][:, h * TC2:(h + 1) * TC2], op=ALU.mult), inc=dv)
            M["du"][(c, h)] = dv.v

        def emit_dexp(c, h):
            if h >= 3:
                p.wait("scalar", dv, M["scan"][(c, h - 3)])
            elif c >= 1:
                p.wait("scalar", dv, M["scan"][(c - 1, G_I - 3 + h)])
            p.wait("scalar", dv, M["sp_add_h"][(c, h)])
            da3 = dam3[(c * G_I + h) % 3]
            for n in range(NST):
                p.add("scalar", lambda s, h=h, n=n, c=c, da3=da3: s.activation(
                    out=da3[:, n, 1:TC2 + 1],
                    in_=deltab[c % 2][:, h * TC2:(h + 1) * TC2],
                    func=AF.Exp, scale=a_sb[:, h * NST + n:h * NST + n + 1]),
                    inc=(ac if n == NST - 1 else None))
            M["exp"][(c, h)] = ac.v

        def emit_scan_iter(c, h):
            if h == 0:
                p.wait("vector", ac, M["bc_ev_all"]["b"])
            p.add("vector", lambda v, h=h: v.tensor_tensor(
                out=xpm3[:, :, 1:TC2 + 1],
                in0=dub.ap()[:, (h % 4) * TC2:(h % 4 + 1) * TC2].unsqueeze(
                    1).broadcast_to([128, NST, TC2]),
                in1=bm3[:, :, 1:TC2 + 1], op=ALU.mult))
            p.add("vector", lambda v, h=h: v.tensor_copy(
                out=xpm3[:, :, 0:1],
                in_=carry.ap()[:, h * NST:(h + 1) * NST].unsqueeze(2)), inc=dv)
            M["xp"][(c, h)] = dv.v
            p.wait("vector", ac, M["exp"][(c, h)])
            if h >= 1:
                p.wait("vector", pe, M["y"][(c, h - 1)])
            p.add("vector", lambda v, da=dam[(c * G_I + h) % 3]: v.tensor_tensor_scan(
                out=hm[:], data0=da[:], data1=xpm[:], initial=0.0,
                op0=ALU.mult, op1=ALU.add), inc=dv)
            M["scan"][(c, h)] = dv.v
            p.add("vector", lambda v, h=h: v.tensor_copy(
                out=carry.ap()[:, h * NST:(h + 1) * NST].unsqueeze(2),
                in_=hm3[:, :, TC2:TC2 + 1]))
            if h == 0:
                p.wait("vector", ac, M["bc_ev_all"]["c"])
            p.add("vector", lambda v: v.tensor_tensor(
                out=hm[:], in0=hm[:], in1=cm[:], op=ALU.mult), inc=dv)
            M["hc"][(c, h)] = dv.v
            if h >= 2:
                emit_gate(c, h - 2)

            p.wait("tensor", dv, M["hc"][(c, h)])
            pyt = ps_y[h % 2]
            if h >= 2:
                p.wait("tensor", dv, M["gate"][(c, h - 2)])
            else:
                p.wait("tensor", ac, M["bc_ev_all"]["c"])
            for n in range(NST):
                p.add("tensor", lambda t, n=n, pyt=pyt: t.matmul(
                    out=pyt[:, 0:TC2], lhsT=idn[:], rhs=hcm3[:, n, 1:TC2 + 1],
                    start=(n == 0), stop=False))
            p.add("tensor", lambda t, h=h, pyt=pyt, c=c: t.matmul(
                out=pyt[:, 0:TC2], lhsT=dd[:, h * 128:(h + 1) * 128],
                rhs=ub[c % 2][:, h * TC2:(h + 1) * TC2], start=False, stop=True),
                inc=pe)
            M["y"][(c, h)] = pe.v

        def emit_gate(c, h):
            p.wait("vector", ac, M["exp"][(c, h)])
            p.wait("vector", pe, M["y"][(c, h)])
            p.add("vector", lambda v, h=h, c=c: v.tensor_tensor(
                out=deltab[c % 2].ap()[:, h * TC2:(h + 1) * TC2],
                in0=ps_y[h % 2][:, 0:TC2],
                in1=sres[c % 2][:, h * TC2:(h + 1) * TC2], op=ALU.mult), inc=dv)
            M["gate"][(c, h)] = dv.v

        def emit_wout_load(c, rnd, s, seg_mm):
            buf = woutb[s % 2]
            if c == 0 and rnd == 0 and s <= 1:
                p.wait("sync", v_wcast[1], v_wcast[2])
            if (rnd, s) == (0, 0) and (c + 1, 7) in M["win_mm"]:
                p.wait("sync", pe, M["win_mm"][(c + 1, 7)])
            if (rnd, s) != (0, 0) or c >= 1:
                key = (rnd, s - 2) if s >= 2 else (rnd - 1, s + 2)
                if key in seg_mm:
                    p.wait("sync", pe, seg_mm[key])
                elif c >= 1:
                    p.wait("sync", pe, M["wout_mm"][c - 1])
            p.add("sync", lambda ss, rnd=rnd, s=s, buf=buf: ss.dma_start(
                out=buf[:], in_=wout_bf[rnd, 4 * s:4 * s + 4].rearrange(
                    "k p c -> p k c")), inc=ld, k=16)
            M["wo_ld"][(c, rnd, s)] = ld.v

        WO_POOL = [(ps_b[0], "b0"), (ps_b[1], "b1"), (ps_bc, "bc"), (ps_sq, "sq")]

        def emit_wout_seg(c, rnd, s, seg_mm):
            pool = WO_POOL
            buf = woutb[s % 2]
            if (c, rnd, s) not in M["wo_ld"]:
                emit_wout_load(c, rnd, s, seg_mm)
            p.wait("tensor", ld, M["wo_ld"][(c, rnd, s)])
            if rnd == 0:
                p.wait("tensor", dv, M["gate"][(c, 4 * s + 3)])
            if rnd == 0 and s == 0:
                p.wait("tensor", dv, M["lnbc"].get(c + 1, M["lnbc"][c]))
                p.wait("tensor", dv, M["sum_rd"].get(c + 1, M["sum_rd"][c]))
                if (c + 1, 7) in M["wdt_pm"]:
                    p.wait("tensor", ac, M["wdt_pm"][(c + 1, 6)])
                    p.wait("tensor", ac, M["wdt_pm"][(c + 1, 7)])
            elif rnd == 1 and s == 0:
                for o in range(4):
                    p.wait("tensor", dv, M["outadd"][(c, pool[o][1])])
            for o in range(4):
                pm = pool[o][0]
                for j in range(4):
                    kk = 4 * s + j
                    p.add("tensor", lambda t, o=o, j=j, kk=kk, buf=buf, pm=pm,
                          s=s: t.matmul(
                        out=pm[:, 0:TC2],
                        lhsT=buf[:, j * 512 + o * 128:j * 512 + (o + 1) * 128],
                        rhs=deltab[c % 2].ap()[:, kk * TC2:(kk + 1) * TC2],
                        start=(s == 0 and j == 0), stop=(s == 3 and j == 3)),
                        inc=(pe if j == 3 else None))
            seg_mm[(rnd, s)] = pe.v

        def emit_wout_tail(c, seg_mm):
            pool = WO_POOL
            for rnd in range(2):
                for s in range(4):
                    if (rnd, s) not in seg_mm:
                        emit_wout_seg(c, rnd, s, seg_mm)
                for o in range(4):
                    pm, pmname = pool[o]
                    oo = rnd * 4 + o
                    p.wait("vector", pe, seg_mm[(rnd, 3)] - (3 - o))
                    p.add("vector", lambda v, oo=oo, pm=pm, c=c: v.tensor_tensor(
                        out=xbuf[c % 2][:, oo * TC2:(oo + 1) * TC2],
                        in0=pm[:, 0:TC2],
                        in1=xbuf[c % 2][:, oo * TC2:(oo + 1) * TC2], op=ALU.add),
                        inc=dv)
                    M["outadd"][(c, pmname)] = dv.v
            M["wout_mm"][c] = pe.v
            v_out = dv.v
            p.wait("sync", dv, v_out)
            p.add("sync", lambda s, c=c: s.dma_start(
                out=xout_t[:, :, c * TC2:(c + 1) * TC2], in_=xbuf[c % 2][:]), inc=st, k=16)
            M["store"] = st.v

        SP_ACTS = {9: (0,), 10: (1, 2), 11: (3, 4), 12: (5, 6), 13: (7,)}
        SP_ADDS = {11: (0,), 12: (1, 2), 13: (3, 4), 14: (5, 6)}

        def pre_slots(c, h):
            if h == 0:
                pre_load(c)
            elif h == 1:
                pre_ln_a(c)
            elif h == 2:
                pre_ln_b(c)
                pre_halo(c)
            elif 3 <= h <= 6:
                pre_win(c, (h - 3) * 2)
                pre_win(c, (h - 3) * 2 + 1)
            elif h in (7, 8):
                pre_conv(c, range((h - 7) * 8, (h - 6) * 8))
            elif h == 9:
                pre_wx(c)
            if h in SP_ACTS:
                pre_sp_act(c, SP_ACTS[h])

        M["bc_ev"] = {}
        M["sp_ln"] = {}
        M["wo_ld"] = {}
        M["sp_add_h"] = {}
        M["bc_ev_all"] = {}
        M["halo"] = 0
        M["wdt_mm_last"] = 0
        M["bc_mm_last"] = 0
        M["wdt_last"] = 0

        for h in range(G_I):
            if h in SP_ADDS:
                pre_sp_add(0, SP_ADDS[h])
            pre_slots(0, h)
        pre_sp_add(0, (7,))
        pre_bcast(0, "b")
        pre_bcast(0, "c")
        for h0 in range(3):
            emit_du(0, h0)
            emit_dexp(0, h0)

        for c in range(NCH2):
            wo_seg = {}
            for h in range(G_I):
                if c + 1 < NCH2 and h in SP_ADDS:
                    pre_sp_add(c + 1, SP_ADDS[h])
                emit_scan_iter(c, h)
                if c + 1 < NCH2:
                    pre_slots(c + 1, h)
                if h in (10, 11):
                    emit_wout_load(c, 0, h - 10, {})
                if h + 3 < G_I:
                    emit_du(c, h + 3)
                    emit_dexp(c, h + 3)
                if h in (13, 14) and c + 1 < NCH2:
                    emit_du(c + 1, h - 13)
                    emit_dexp(c + 1, h - 13)
                if h == 14:
                    for s0 in range(3):
                        emit_wout_seg(c, 0, s0, wo_seg)
            if c + 1 < NCH2:
                pre_sp_add(c + 1, (7,))
            emit_gate(c, G_I - 2)
            emit_gate(c, G_I - 1)
            emit_wout_tail(c, wo_seg)
            if c + 1 < NCH2:
                pre_bcast(c + 1, "b")
                pre_bcast(c + 1, "c")
                emit_du(c + 1, 2)
                emit_dexp(c + 1, 2)

        p.wait("sync", st, st.v)



def build_program(emit=("p1", "p2", "p3")):
    nc = bass.Bass("TRN2", target_bir_lowering=False, debug=False)
    inp = {}

    def ein(name, shape):
        inp[name] = nc.dram_tensor(name, shape, F32, kind="ExternalInput").ap()

    ein("x0", [D_MODEL, T])
    ein("wrT", [D_MODEL, D_MODEL])
    ein("wkT", [D_MODEL, D_MODEL])
    ein("wvT", [D_MODEL, D_MODEL])
    ein("mu", [128, 16])
    ein("identity", [128, 128])
    ein("ones_c", [128, 1])
    ein("ones_r", [1, 128])
    ein("win_t", [8, 8, 128, 512])
    ein("wout_t", [2, 16, 128, 512])
    ein("wx", [D_IN, 96])
    ein("wdt", [DT_RANK, D_IN])
    ein("a_log", [128, G_I * NST])
    ein("convw", [128, 64])
    ein("convb", [128, G_I])
    ein("bdt", [128, G_I])
    ein("bdt_r", [1, D_IN])
    ein("dvec", [128, NST])
    ein("selb", [16, 16 * 128])

    out = nc.dram_tensor("out", [D_MODEL, T], F32, kind="ExternalOutput").ap()
    if _DBG:
        dbg = {}
        for nm, shape in (("xnx", [128, G_M * (TC1 + 1)]), ("xrb", [128, G_M * TC1]),
                          ("xkb", [128, G_M * TC1]), ("rb", [128, G_M * TC1]),
                          ("kab", [128, G_M * TC1])):
            dbg[nm] = nc.dram_tensor("dbg_" + nm, shape, F32, kind="ExternalOutput").ap()
        for nm, shape in (("a_sb", [128, G_I * NST]), ("deltab", [128, G_I * TC2]),
                          ("ub", [128, G_I * TC2]), ("sres", [128, G_I * TC2]),
                          ("carry", [128, G_I * NST]), ("dam", [128, FMRG]),
                          ("bm", [128, FMRG]), ("cm", [128, FMRG]),
                          ("dub", [128, G_I * TC2]), ("xceD", [128, G_I * (TC2 + 3)])):
            dbg[nm] = nc.dram_tensor("dbgm_" + nm, shape, F32, kind="ExternalOutput").ap()
        inp["_dbg"] = dbg
        dbs = {}
        for nm, shape, dt in (("s_dam", [128, FMRG], BF16), ("s_xpm", [128, FMRG], BF16),
                              ("s_hm", [128, FMRG], BF16), ("s_carry", [128, G_I * NST], F32),
                              ("s_dain", [128, FMRG], BF16)):
            dbs[nm] = nc.dram_tensor("dbs_" + nm, shape, dt, kind="ExternalOutput").ap()
        inp["_dbgs"] = dbs
    xkind = "ExternalOutput" if _DBG else "Internal"
    x1 = nc.dram_tensor("x1_scr", [D_MODEL, T], F32, kind=xkind).ap()
    x2 = nc.dram_tensor("x2_scr", [D_MODEL, T], F32, kind=xkind).ap()
    win_bf = nc.dram_tensor("win_bf", [8, 8, 128, 512], BF16, kind="Internal").ap()
    wout_bf = nc.dram_tensor("wout_bf", [2, 16, 128, 512], BF16, kind="Internal").ap()
    scr = {"win_bf": win_bf, "wout_bf": wout_bf}

    p = Prog()
    with ExitStack() as es:
        sems = {k: Ctr(es, nc, k) for k in ("ld", "lg", "pe", "dv", "ac", "st", "gp", "bar")}
        block = es.enter_context(nc.Block())

        chain = []
        if "p1" in emit:
            chain.append(("ffn", inp["x0"], x1 if ("p2" in emit or "p3" in emit) else out, "f1"))
        if "p2" in emit:
            src = x1 if "p1" in emit else inp["x0"]
            chain.append(("mamba", src, x2 if "p3" in emit else out, "m"))
        if "p3" in emit:
            src = x2 if "p2" in emit else (x1 if "p1" in emit else inp["x0"])
            chain.append(("ffn", src, out, "f3"))

        wc_stack = ExitStack()
        need_wcast = any(k == "mamba" for k, _, _, _ in chain)
        if need_wcast and chain[0][0] == "mamba":
            emit_wcast(nc, p, sems, inp, scr, wc_stack)
        for i, (kind, xi, xo, tag) in enumerate(chain):
            if i > 0:
                wc_stack.close()
                p.barrier(sems["bar"])
            if kind == "ffn":
                emit_ffn_pass(nc, p, sems, inp, xi, xo, tag)
            else:
                emit_mamba_pass(nc, p, sems, inp, xi, xo, scr)
            if i == 0 and need_wcast and kind == "ffn":
                emit_wcast(nc, p, sems, inp, scr, wc_stack)
        wc_stack.close()

        p.finalize(block)
    return nc



def _host_inputs(inputs):
    x = np.ascontiguousarray(np.transpose(np.asarray(inputs["L0_attn_x"]), (0, 2, 1)))
    f32 = np.float32

    def t2(a):
        return np.ascontiguousarray(np.asarray(a).T.astype(f32, copy=False))

    wrT = t2(inputs["L0_ffn_W_r"])
    wkT = t2(inputs["L0_ffn_W_k"])
    wvT = t2(inputs["L0_ffn_W_v"])

    mu = np.zeros((128, 16), f32)
    mu[:, 0:8] = np.asarray(inputs["L0_ffn_mu_r"]).reshape(8, 128).T
    mu[:, 8:16] = np.asarray(inputs["L0_ffn_mu_k"]).reshape(8, 128).T

    identity = np.eye(128, dtype=f32)
    ones_c = np.ones((128, 1), f32)
    ones_r = np.ones((1, 128), f32)

    win = np.asarray(inputs["L0_attn_Win"]).astype(f32, copy=False)
    win_t = np.zeros((8, 8, 128, 512), f32)
    for grp in range(8):
        for k in range(8):
            win_t[grp, k] = win[k * 128:(k + 1) * 128, grp * 512:(grp + 1) * 512]

    wout = np.asarray(inputs["L0_attn_Wout"]).astype(f32, copy=False)
    wout_t = np.zeros((2, 16, 128, 512), f32)
    for grp in range(2):
        for k in range(16):
            wout_t[grp, k] = wout[k * 128:(k + 1) * 128, grp * 512:(grp + 1) * 512]

    wx = np.ascontiguousarray(np.asarray(inputs["L0_attn_Wx"]).astype(f32, copy=False))
    wdt = np.ascontiguousarray(np.asarray(inputs["L0_attn_Wdt"]).astype(f32, copy=False))

    a_log = np.asarray(inputs["L0_attn_A_log"]).astype(f32, copy=False)
    a_log_t = np.zeros((128, G_I * NST), f32)
    for h in range(G_I):
        a_log_t[:, h * NST:(h + 1) * NST] = a_log[h * 128:(h + 1) * 128, :]

    convw = np.asarray(inputs["L0_attn_conv_w"]).astype(f32, copy=False).reshape(4, D_IN)
    convw_t = np.zeros((128, 64), f32)
    for h in range(G_I):
        for k in range(4):
            convw_t[:, h * 4 + k] = convw[k, h * 128:(h + 1) * 128]

    def cols(vec, g):
        return np.ascontiguousarray(np.asarray(vec).reshape(g, 128).T.astype(f32, copy=False))

    convb = cols(inputs["L0_attn_conv_b"], G_I)
    bdt = cols(inputs["L0_attn_bdt"], G_I)
    bdt_r = np.ascontiguousarray(np.asarray(inputs["L0_attn_bdt"]).astype(f32).reshape(1, D_IN))
    dvec = cols(inputs["L0_attn_D"], G_I)

    selb = np.zeros((16, 16 * 128), f32)
    for n in range(16):
        selb[n, n * 128:(n + 1) * 128] = 1.0

    shared = dict(wrT=wrT, wkT=wkT, wvT=wvT, mu=mu, identity=identity,
                  ones_c=ones_c, ones_r=ones_r, win_t=win_t, wout_t=wout_t,
                  wx=wx, wdt=wdt, a_log=a_log_t, convw=convw_t, convb=convb,
                  bdt=bdt, bdt_r=bdt_r, dvec=dvec, selb=selb)
    in_maps = [dict(shared, x0=np.ascontiguousarray(x[b])) for b in range(8)]
    return in_maps


def kernel(**inputs):
    in_maps = _host_inputs(inputs)
    nc = build_program(_EMIT)
    res = run_bass_kernel_spmd(nc, in_maps, core_ids=list(range(8)))
    outs = [r["out"] for r in res.results]
    y = np.stack(outs, axis=0)
    return np.ascontiguousarray(np.transpose(y, (0, 2, 1)).astype(np.float32))
